# version 04
# baseline (speedup 1.0000x reference)
import contextlib

import numpy as np
import ml_dtypes

import concourse.bass as bass
import concourse.mybir as mybir
from concourse.bass_utils import run_bass_kernel_spmd
from concourse.aot_env import allow_ici_all_to_all

F32 = mybir.dt.float32
BF16 = mybir.dt.bfloat16
AF = mybir.ActivationFunctionType
OP = mybir.AluOpType
AX = mybir.AxisListType

R = 8
D = 2048
DI, NST, DTR = 4096, 16, 128
T = 16384
CH = DI // R
TOK = T // R
TC = 512
NCHUNK = T // TC
SC = 256
NSC = T // SC
INTER = 8192
EPS = 1e-5
RSD = float(1.0 / np.sqrt(D))
RSI = float(1.0 / np.sqrt(INTER))

DBG = False


def build_nc():
    nc = bass.Bass("TRN2", target_bir_lowering=False)

    x_full = nc.dram_tensor("x_full", [T, D], F32, kind="ExternalInput").ap()
    x_my = nc.dram_tensor("x_my", [TOK, D], F32, kind="ExternalInput").ap()
    w_in = nc.dram_tensor("w_in", [D, 2 * CH], F32, kind="ExternalInput").ap()
    conv_w = nc.dram_tensor("conv_w", [128, 4, 4], F32, kind="ExternalInput").ap()
    conv_b = nc.dram_tensor("conv_b", [128, 4], F32, kind="ExternalInput").ap()
    w_xp = nc.dram_tensor("w_xp", [CH, 160], F32, kind="ExternalInput").ap()
    w_dt = nc.dram_tensor("w_dt", [DTR, CH], F32, kind="ExternalInput").ap()
    b_dt = nc.dram_tensor("b_dt", [128, 4], F32, kind="ExternalInput").ap()
    a_log = nc.dram_tensor("a_log", [128, 4, NST], F32, kind="ExternalInput").ap()
    d_vec = nc.dram_tensor("d_vec", [128, 4], F32, kind="ExternalInput").ap()
    w_norm = nc.dram_tensor("w_norm", [128, 16], F32, kind="ExternalInput").ap()
    w_out = nc.dram_tensor("w_out", [DI, D], F32, kind="ExternalInput").ap()
    w_up_sh = nc.dram_tensor("w_up_sh", [INTER // R, D], F32, kind="ExternalInput").ap()
    w_gate_sh = nc.dram_tensor("w_gate_sh", [INTER // R, D], F32, kind="ExternalInput").ap()
    w_down_sh = nc.dram_tensor("w_down_sh", [D // R, INTER], F32, kind="ExternalInput").ap()
    ws_ug_sh = nc.dram_tensor("ws_ug_sh", [128, 8], F32, kind="ExternalInput").ap()
    ws_gt_sh = nc.dram_tensor("ws_gt_sh", [128, 8], F32, kind="ExternalInput").ap()
    ws_dp_sh = nc.dram_tensor("ws_dp_sh", [128, 8], F32, kind="ExternalInput").ap()
    su_ug = nc.dram_tensor("su_ug", [128, 16], F32, kind="ExternalInput").ap()
    su_dp = nc.dram_tensor("su_dp", [128, 64], F32, kind="ExternalInput").ap()
    identb_d = nc.dram_tensor("identb", [128, 128], BF16, kind="ExternalInput").ap()
    identf_d = nc.dram_tensor("identf", [128, 128], F32, kind="ExternalInput").ap()

    out_tok = nc.dram_tensor("out_tok", [TOK, D], F32, kind="ExternalOutput").ap()

    xdbc_in = nc.dram_tensor("xdbc_in", [160, T], F32)
    xdbc_out = nc.dram_tensor("xdbc_out", [160, T], F32, addr_space="Shared")
    u_sp = nc.dram_tensor("u_sp", [128, 4, T], F32)
    sr_sp = nc.dram_tensor("sr_sp", [128, 4, T], F32)
    ya_in = nc.dram_tensor("ya_in", [R * CH, TOK], BF16)
    ya_out = nc.dram_tensor("ya_out", [R * CH, TOK], BF16)
    wug_in = nc.dram_tensor("wug_in", [D, 2 * INTER // R], BF16)
    wug_out = nc.dram_tensor("wug_out", [R * D, 2 * INTER // R], BF16, addr_space="Shared")
    wdn_in = nc.dram_tensor("wdn_in", [INTER, D // R], BF16)
    wdn_out = nc.dram_tensor("wdn_out", [R * INTER, D // R], BF16, addr_space="Shared")
    wout_bf_d = nc.dram_tensor("wout_bf_d", [DI, D], BF16)

    dbg = {}
    if DBG:
        dbg["u"] = nc.dram_tensor("dbg_u", [128, 4, T], F32, kind="ExternalOutput").ap()
        dbg["sr"] = nc.dram_tensor("dbg_sr", [128, 4, T], F32, kind="ExternalOutput").ap()
        dbg["xdbc"] = nc.dram_tensor("dbg_xdbc", [160, T], F32, kind="ExternalOutput").ap()
        dbg["ya"] = nc.dram_tensor("dbg_ya", [R * CH, TOK], BF16, kind="ExternalOutput").ap()
        dbg["x1"] = nc.dram_tensor("dbg_x1", [TOK, D], F32, kind="ExternalOutput").ap()
        dbg["mo"] = nc.dram_tensor("dbg_mo", [TOK, D], F32, kind="ExternalOutput").ap()
        dbg["wout"] = nc.dram_tensor("dbg_wout", [DI, D], BF16, kind="ExternalOutput").ap()
        dbg["stA"] = nc.dram_tensor("dbg_stA", [4, 16, 128, 512], F32, kind="ExternalOutput").ap()
        dbg["wug"] = nc.dram_tensor("dbg_wug", [R * D, 2 * INTER // R], BF16, kind="ExternalOutput").ap()
        dbg["wdn"] = nc.dram_tensor("dbg_wdn", [R * INTER, D // R], BF16, kind="ExternalOutput").ap()
        dbg["xf"] = nc.dram_tensor("dbg_xf", [4, 128, 16, 512], BF16, kind="ExternalOutput").ap()
        dbg["mid"] = nc.dram_tensor("dbg_mid", [4, 128, 64, 512], BF16, kind="ExternalOutput").ap()
        dbg["fo"] = nc.dram_tensor("dbg_fo", [TOK, D], F32, kind="ExternalOutput").ap()

    es = contextlib.ExitStack()

    def sb(name, shape, dt):
        return es.enter_context(nc.sbuf_tensor(name, shape, dt))

    def pt(name, shape, dt):
        return es.enter_context(nc.psum_tensor(name, shape, dt))

    cnt = {}
    sems = {}

    with es:
        for s_ in ["DS", "DG", "VS", "SS", "TS", "GS", "CC", "VE", "SE"]:
            sems[s_] = es.enter_context(nc.semaphore(s_))
            cnt[s_] = 0

        def vedge(v, instr):
            cnt["VE"] += 1
            instr.then_inc(sems["VE"], 1)
            v.wait_ge(sems["VE"], cnt["VE"])

        def sedge(se, instr):
            cnt["SE"] += 1
            instr.then_inc(sems["SE"], 1)
            se.wait_ge(sems["SE"], cnt["SE"])

        def inc(instr, name, dma=False):
            if name == "CC":
                instr.then_inc(sems["CC"])
                cnt["CC"] += 1
            else:
                v = 16 if dma else 1
                instr.then_inc(sems[name], v)
                cnt[name] += v
            return cnt[name]

        identb = sb("identb_s", [128, 128], BF16)
        identf = sb("identf_s", [128, 128], F32)
        wxp_bf = sb("wxp_bf", [128, 4, 160], F32)
        wdt_bf = sb("wdt_bf", [128, CH], BF16)
        a_col = sb("a_col", [128, 4, NST], F32)
        d_col = sb("d_col", [128, 4], F32)
        cb_col = sb("cb_col", [128, 4], F32)
        cw_col = sb("cw_col", [128, 4, 4], F32)
        bdt_col = sb("bdt_col", [128, 4], F32)
        suug_col = sb("suug_col", [128, 16], F32)
        sudp_col = sb("sudp_col", [128, 64], F32)
        ones_col = sb("ones_col", [128, 1], F32)
        negone_col = sb("negone_col", [128, 1], F32)
        hcarry = sb("hcarry", [128, 4, NST], F32)
        wn_c = sb("wn_c", [128, 16], F32)
        sqs = sb("sqs", [128, D], F32)
        sums_c = sb("sums_c", [128, 4], F32)
        ssq_c = sb("ssq_c", [128, 4], F32)
        negmean_c = sb("negmean_c", [128, 4], F32)
        var_c = sb("var_c", [128, 4], F32)
        t1_c = sb("t1_c", [128, 4], F32)
        t2_c = sb("t2_c", [128, 4], F32)
        sroot_c = sb("sroot_c", [128, 4], F32)
        stok_c = sb("stok_c", [128, 4], F32)
        btok_c = sb("btok_c", [128, 4], F32)

        PSA = pt("PSA", [128, 512], F32)
        PSB = pt("PSB", [128, 512], F32)
        PSC = pt("PSC", [128, 512], F32)
        PSD = pt("PSD", [128, 512], F32)
        PSE = pt("PSE", [128, 512], F32)
        PSX = pt("PSX", [32, 512], F32)
        PSBF = pt("PSBF", [128, 2048], BF16)

        es_wf = contextlib.ExitStack()
        win_bf = es_wf.enter_context(nc.sbuf_tensor("win_bf", [128, 16, 2 * CH], BF16))
        with (
            nc.sbuf_tensor("wt_a", [128, 8, D], F32) as wt_a,
            nc.sbuf_tensor("wt_b", [128, 8, D], F32) as wt_b,
            nc.sbuf_tensor("wt_bf", [128, 8, D], BF16) as wt_bf,
            nc.sbuf_tensor("wt_st", [128, 1024], BF16) as wt_st,
            nc.sbuf_tensor("wsc_c", [128, 8], F32) as wsc_c,
            nc.Block() as blk,
        ):
            @blk.sync
            def _(sy):
                inc(sy.dma_start(out=identb[:, :], in_=identb_d), "DS", dma=True)
                inc(sy.dma_start(out=identf[:, :], in_=identf_d), "DS", dma=True)
                inc(sy.dma_start(out=cb_col[:, :], in_=conv_b), "DS", dma=True)
                inc(sy.dma_start(out=cw_col[:, :, :], in_=conv_w), "DS", dma=True)
                inc(sy.dma_start(out=d_col[:, :], in_=d_vec), "DS", dma=True)
                inc(sy.dma_start(out=bdt_col[:, :], in_=b_dt), "DS", dma=True)
                inc(sy.dma_start(out=suug_col[:, :], in_=su_ug), "DS", dma=True)
                inc(sy.dma_start(out=sudp_col[:, :], in_=su_dp), "DS", dma=True)
                inc(sy.dma_start(out=a_col[:, :, :], in_=a_log), "DS", dma=True)
                inc(sy.dma_start(out=wn_c[:, :], in_=w_norm), "DS", dma=True)

            @blk.gpsimd
            def _(g):
                inc(g.memset(ones_col[:, :], 1.0), "GS")
                inc(g.memset(negone_col[:, :], -1.0), "GS")
                inc(g.dma_start(out=wxp_bf[:, :, :], in_=w_xp.rearrange("(kt p) m -> p kt m", p=128)), "DG", dma=True)
                inc(g.dma_start(out=wdt_bf[:, :], in_=w_dt), "DG", dma=True)
                inc(g.dma_start(out=wout_bf_d.ap(), in_=w_out), "DG", dma=True)

            ld_all_ds = cnt["DS"]
            ld_all_dg = cnt["DG"]
            gs_all = cnt["GS"]

            @blk.scalar
            def _(s):
                s.wait_ge(sems["DS"], ld_all_ds)
                s.wait_ge(sems["DG"], ld_all_dg)
                s.wait_ge(sems["GS"], gs_all)
                sedge(s, s.activation(out=a_col[:, :, :], in_=a_col[:, :, :], func=AF.Exp))
                inc(s.mul(out=a_col[:, :, :], in_=a_col[:, :, :], mul=negone_col[:, :]), "SS")

            prev_scale_vs = [0]
            evac_half_ss = {0: 0, 1: 0}
            st_prev_wt = [0]

            def transform(src_ap, wsc_ap, nrt, nfree, dst_dram, col0, ncols, rsn):
                wv_a = wt_a.ap().rearrange("p a b -> p (a b)").rearrange("p (rt m) -> p rt m", rt=nrt)
                wv_b = wt_b.ap().rearrange("p a b -> p (a b)").rearrange("p (rt m) -> p rt m", rt=nrt)
                wv_bf = wt_bf.ap().rearrange("p a b -> p (a b)").rearrange("p (rt m) -> p rt m", rt=nrt)

                @blk.sync
                def _(sy):
                    sy.wait_ge(sems["VS"], prev_scale_vs[0])
                    inc(sy.dma_start(out=wv_a,
                                     in_=src_ap.rearrange("(rt p) m -> p rt m", p=128)), "DS", dma=True)
                    inc(sy.dma_start(out=wsc_c[:, 0:nrt],
                                     in_=wsc_ap[:, 0:nrt]), "DS", dma=True)
                ld_here = cnt["DS"]

                @blk.vector
                def _(v):
                    v.wait_ge(sems["DS"], ld_here)
                    cur, oth = wv_a, wv_b
                    h = 1
                    while h < nfree:
                        va = cur.rearrange(
                            "p rt (nb two h) -> p rt nb two h", two=2, h=h)
                        vo = oth.rearrange(
                            "p rt (nb two h) -> p rt nb two h", two=2, h=h)
                        v.tensor_tensor(vo[:, :, :, 0, :], va[:, :, :, 0, :], va[:, :, :, 1, :], OP.add)
                        v.tensor_tensor(vo[:, :, :, 1, :], va[:, :, :, 0, :], va[:, :, :, 1, :], OP.subtract)
                        cur, oth = oth, cur
                        h *= 2
                    for rt in range(nrt):
                        i = v.tensor_scalar(out=wv_bf[:, rt, :], in0=cur[:, rt, :],
                                            scalar1=wsc_c[:, rt:rt + 1], scalar2=rsn,
                                            op0=OP.mult, op1=OP.mult)
                    inc(i, "VS")
                vs_here = cnt["VS"]
                prev_scale_vs[0] = vs_here

                ndt = nfree // 128
                for dt in range(ndt):
                    half = dt % 2

                    @blk.tensor
                    def _(t):
                        if dt == 0:
                            t.wait_ge(sems["VS"], vs_here)
                        t.wait_ge(sems["SS"], evac_half_ss[half])
                        for rt in range(nrt):
                            i = t.transpose(
                                PSBF[:, half * 1024 + rt * 128: half * 1024 + (rt + 1) * 128],
                                wv_bf[:, rt, dt * 128:(dt + 1) * 128], identb[:, :])
                        inc(i, "TS")
                    ts_here = cnt["TS"]

                    @blk.scalar
                    def _(s):
                        s.wait_ge(sems["TS"], ts_here)
                        s.wait_ge(sems["DS"], st_prev_wt[0])
                        inc(s.copy(out=wt_st[:, 0:nrt * 128],
                                   in_=PSBF[:, half * 1024:half * 1024 + nrt * 128]), "SS")
                    evac_half_ss[half] = cnt["SS"]
                    ss_here = cnt["SS"]

                    @blk.sync
                    def _(sy):
                        sy.wait_ge(sems["SS"], ss_here)
                        inc(sy.dma_start(out=dst_dram[dt * 128:(dt + 1) * 128, col0:col0 + ncols],
                                         in_=wt_st[:, 0:ncols]), "DS", dma=True)
                    st_prev_wt[0] = cnt["DS"]

            transform(w_up_sh, ws_ug_sh, 8, D, wug_in, 0, 1024, RSD)
            transform(w_gate_sh, ws_gt_sh, 8, D, wug_in, 1024, 1024, RSD)
            transform(w_down_sh, ws_dp_sh, 2, INTER, wdn_in, 0, 256, RSI)
            st_wtr = cnt["DS"]

            @blk.gpsimd
            def _(g):
                g.wait_ge(sems["DS"], st_wtr)
                inc(g.collective_compute("AllGather", OP.bypass, replica_groups=[list(range(R))],
                                         ins=[wug_in.ap()], outs=[wug_out.ap()]), "CC")
                inc(g.collective_compute("AllGather", OP.bypass, replica_groups=[list(range(R))],
                                         ins=[wdn_in.ap()], outs=[wdn_out.ap()]), "CC")
            cc_wag = cnt["CC"]

            @blk.sync
            def _(sy):
                sy.wait_ge(sems["VS"], prev_scale_vs[0])
                inc(sy.dma_start(out=wt_a.ap().rearrange("p rt (a m) -> p (rt a) m", a=2),
                                 in_=w_in.rearrange("(kt p) m -> p kt m", p=128)), "DS", dma=True)
            ld_win = cnt["DS"]

            @blk.vector
            def _(v):
                v.wait_ge(sems["DS"], ld_win)
                wv = wt_a.ap().rearrange("p rt (a m) -> p (rt a) m", a=2)
                for kt in range(16):
                    i = v.tensor_scalar(out=win_bf[:, kt, :], in0=wv[:, kt, :],
                                        scalar1=wn_c[:, kt:kt + 1], scalar2=None, op0=OP.mult)
                inc(i, "VS")
            vs_win = cnt["VS"]

        with (
            nc.sbuf_tensor("xtm0", [128, 4, D], F32) as xtm0,
            nc.sbuf_tensor("xtm1", [128, 4, D], F32) as xtm1,
            nc.sbuf_tensor("xn_tm", [128, 4, D], BF16) as xn_tm,
            nc.sbuf_tensor("xn_fm", [128, 16, TC], BF16) as xn_fm,
            nc.sbuf_tensor("xz0", [128, 4, TC + 3], F32) as xz0,
            nc.sbuf_tensor("xz1", [128, 4, TC + 3], F32) as xz1,
            nc.sbuf_tensor("convt", [128, 4, TC], F32) as convt,
            nc.sbuf_tensor("u_st", [128, 4, TC], F32) as u_st,
            nc.sbuf_tensor("sr_st", [128, 4, TC], F32) as sr_st,
            nc.sbuf_tensor("xdb_st1", [128, TC], F32) as xdb_st1,
            nc.sbuf_tensor("xdb_st2", [32, TC], F32) as xdb_st2,
            nc.Block() as blk,
        ):
            red_ms = [0] * NCHUNK
            apply_ms = [0] * NCHUNK
            transp_ms = [0] * NCHUNK
            half_evac_ss = {0: 0, 1: 0}
            evacfm_ms = [0] * NCHUNK
            psab_evac = {0: 0, 1: 0}
            conv_ms = [[0] * 4 for _ in range(NCHUNK)]
            usilu_ms = [[0] * 4 for _ in range(NCHUNK)]
            uspill_ms = [0] * NCHUNK
            srspill_ms = [0] * NCHUNK
            xdbst_ms = [0] * NCHUNK
            xz_ready_ms = [[0] * 4 for _ in range(NCHUNK)]
            tail_ms = [0] * NCHUNK
            xp_prev_ts = [0]
            psex_prev_ss = [0]

            for k in range(NCHUNK):
                xtm = (xtm0, xtm1)[k % 2]
                xz = (xz0, xz1)[k % 2]
                xzp = (xz0, xz1)[(k + 1) % 2]
                g0 = k * TC

                @blk.gpsimd
                def _(g):
                    if k >= 2:
                        g.wait_ge(sems["SS"], apply_ms[k - 2])
                        g.wait_ge(sems["VS"], red_ms[k - 2])
                    inc(g.dma_start(out=xtm[:, :, :],
                                    in_=x_full[g0:g0 + TC, :].rearrange("(tt p) d -> p tt d", p=128)),
                        "DG", dma=True)
                ld_x = cnt["DG"]

                @blk.vector
                def _(v):
                    v.wait_ge(sems["DG"], ld_x)
                    if k == 0:
                        v.wait_ge(sems["VS"], vs_win)
                    for tt in range(4):
                        v.reduce_sum(sums_c[:, tt:tt + 1], xtm[:, tt, :], axis=AX.X)
                    for tt in range(4):
                        v.tensor_mul(sqs[:, :], xtm[:, tt, :], xtm[:, tt, :])
                        i = v.reduce_sum(ssq_c[:, tt:tt + 1], sqs[:, :], axis=AX.X)
                    red_ms[k] = inc(i, "VS")

                @blk.vector
                def _(v):
                    vedge(v, v.tensor_scalar(out=negmean_c[:, :], in0=sums_c[:, :], scalar1=-1.0 / D,
                                             scalar2=None, op0=OP.mult))
                    vedge(v, v.tensor_mul(t1_c[:, :], negmean_c[:, :], negmean_c[:, :]))
                    vedge(v, v.tensor_scalar(out=var_c[:, :], in0=ssq_c[:, :], scalar1=1.0 / D,
                                             scalar2=None, op0=OP.mult))
                    vedge(v, v.tensor_sub(var_c[:, :], var_c[:, :], t1_c[:, :]))
                    vedge(v, v.tensor_scalar(out=t1_c[:, :], in0=var_c[:, :], scalar1=EPS, scalar2=None, op0=OP.add))
                    vedge(v, v.reciprocal(t2_c[:, :], t1_c[:, :]))
                    vedge(v, v.tensor_mul(t2_c[:, :], var_c[:, :], t2_c[:, :]))
                    vedge(v, v.tensor_scalar(out=t2_c[:, :], in0=t2_c[:, :], scalar1=EPS, scalar2=None, op0=OP.add))
                    inc(v.tensor_mul(t1_c[:, :], t1_c[:, :], t2_c[:, :]), "VS")
                vs_d = cnt["VS"]

                @blk.scalar
                def _(s):
                    s.wait_ge(sems["VS"], vs_d)
                    inc(s.activation(out=sroot_c[:, :], in_=t1_c[:, :], func=AF.Sqrt), "SS")
                ss_rt = cnt["SS"]

                @blk.vector
                def _(v):
                    v.wait_ge(sems["SS"], ss_rt)
                    if k >= 1:
                        v.wait_ge(sems["SS"], apply_ms[k - 1])
                    vedge(v, v.reciprocal(stok_c[:, :], sroot_c[:, :]))
                    inc(v.tensor_mul(btok_c[:, :], negmean_c[:, :], stok_c[:, :]), "VS")
                vs_tok = cnt["VS"]

                @blk.scalar
                def _(s):
                    s.wait_ge(sems["VS"], vs_tok)
                    if k >= 1:
                        s.wait_ge(sems["TS"], transp_ms[k - 1])
                    for tt in range(4):
                        i = s.activation(out=xn_tm[:, tt, :], in_=xtm[:, tt, :], func=AF.Identity,
                                         scale=stok_c[:, tt:tt + 1], bias=btok_c[:, tt:tt + 1])
                    apply_ms[k] = inc(i, "SS")

                for tt in range(4):
                    for hh in range(2):
                        half = (tt * 2 + hh) % 2

                        @blk.tensor
                        def _(t):
                            if tt == 0 and hh == 0:
                                t.wait_ge(sems["SS"], apply_ms[k])
                            t.wait_ge(sems["SS"], half_evac_ss[half])
                            for dt in range(hh * 8, hh * 8 + 8):
                                i = t.transpose(
                                    PSBF[:, half * 1024 + (dt - hh * 8) * 128:
                                         half * 1024 + (dt - hh * 8 + 1) * 128],
                                    xn_tm[:, tt, dt * 128:(dt + 1) * 128], identb[:, :])
                            inc(i, "TS")
                        ts_here = cnt["TS"]

                        @blk.scalar
                        def _(s):
                            s.wait_ge(sems["TS"], ts_here)
                            inc(s.copy(out=xn_fm[:, hh * 8:hh * 8 + 8, tt * 128:(tt + 1) * 128],
                                       in_=PSBF.ap()[:, half * 1024:half * 1024 + 1024].rearrange(
                                           "p (dt q) -> p dt q", q=128)), "SS")
                        half_evac_ss[half] = cnt["SS"]
                transp_ms[k] = cnt["TS"]
                evacfm_ms[k] = cnt["SS"]

                for mt in range(8):
                    ps = (PSA, PSB)[mt % 2]

                    @blk.tensor
                    def _(t):
                        if mt == 0:
                            t.wait_ge(sems["SS"], evacfm_ms[k])
                        t.wait_ge(sems["SS"], psab_evac[mt % 2])
                        for kt in range(16):
                            i = t.matmul(ps[:, :], win_bf[:, kt, mt * 128:(mt + 1) * 128],
                                         xn_fm[:, kt, :], start=(kt == 0), stop=(kt == 15))
                        inc(i, "TS")
                    ts_mm = cnt["TS"]

                    @blk.scalar
                    def _(s):
                        s.wait_ge(sems["TS"], ts_mm)
                        if mt < 4:
                            if k >= 2:
                                s.wait_ge(sems["VS"], conv_ms[k - 2][mt])
                            i = s.copy(out=xz[:, mt, 3:TC + 3], in_=ps[:, :])
                            xz_ready_ms[k][mt] = inc(i, "SS")
                        else:
                            ct = mt - 4
                            if k >= 1:
                                s.wait_ge(sems["DG"], srspill_ms[k - 1])
                            i = s.activation(out=sr_st[:, ct, :], in_=ps[:, :], func=AF.Silu)
                            inc(i, "SS")
                    psab_evac[mt % 2] = cnt["SS"]
                ss_sr = cnt["SS"]

                if k % (NCHUNK // 2) == 0:
                    @blk.gpsimd
                    def _(g):
                        if k >= 2:
                            g.wait_ge(sems["VS"], conv_ms[k - 2][3])
                        if k >= 1:
                            g.wait_ge(sems["SS"], tail_ms[k - 1])
                        inc(g.memset(xz[:, :, 0:3], 0.0), "GS")
                    gs_tail = cnt["GS"]
                    tail_is_gs = True
                else:
                    @blk.scalar
                    def _(s):
                        i = s.copy(out=xz[:, :, 0:3], in_=xzp[:, :, TC:TC + 3])
                        tail_ms[k] = inc(i, "SS")
                    gs_tail = None
                    tail_is_gs = False

                for ct in range(4):
                    @blk.vector
                    def _(v):
                        v.wait_ge(sems["SS"], xz_ready_ms[k][ct])
                        if tail_is_gs:
                            v.wait_ge(sems["GS"], gs_tail)
                        else:
                            v.wait_ge(sems["SS"], tail_ms[k])
                        if k >= 1:
                            v.wait_ge(sems["SS"], usilu_ms[k - 1][ct])
                        v.tensor_scalar(out=convt[:, ct, :], in0=xz[:, ct, 0:TC],
                                        scalar1=cw_col[:, ct, 0:1], scalar2=None, op0=OP.mult)
                        for kk in range(1, 4):
                            i = v.scalar_tensor_tensor(out=convt[:, ct, :], in0=xz[:, ct, kk:kk + TC],
                                                       scalar=cw_col[:, ct, kk:kk + 1],
                                                       in1=convt[:, ct, :], op0=OP.mult, op1=OP.add)
                        conv_ms[k][ct] = inc(i, "VS")

                    @blk.scalar
                    def _(s):
                        s.wait_ge(sems["VS"], conv_ms[k][ct])
                        if k >= 1:
                            s.wait_ge(sems["DG"], uspill_ms[k - 1])
                            s.wait_ge(sems["TS"], xp_prev_ts[0])
                        i = s.activation(out=u_st[:, ct, :], in_=convt[:, ct, :], func=AF.Silu,
                                         bias=cb_col[:, ct:ct + 1])
                        usilu_ms[k][ct] = inc(i, "SS")
                ss_u = cnt["SS"]

                @blk.gpsimd
                def _(g):
                    g.wait_ge(sems["SS"], ss_u)
                    inc(g.dma_start(out=u_sp[:, :, g0:g0 + TC], in_=u_st[:, :, :]), "DG", dma=True)
                    uspill_ms[k] = cnt["DG"]
                    g.wait_ge(sems["SS"], ss_sr)
                    inc(g.dma_start(out=sr_sp[:, :, g0:g0 + TC], in_=sr_st[:, :, :]), "DG", dma=True)
                    srspill_ms[k] = cnt["DG"]

                @blk.tensor
                def _(t):
                    t.wait_ge(sems["SS"], ss_u)
                    t.wait_ge(sems["SS"], psex_prev_ss[0])
                    for kt in range(4):
                        t.matmul(PSE[:, :], wxp_bf[:, kt, 0:128], u_st[:, kt, :],
                                 start=(kt == 0), stop=(kt == 3))
                    for kt in range(4):
                        i = t.matmul(PSX[:, :], wxp_bf[:, kt, 128:160], u_st[:, kt, :],
                                     start=(kt == 0), stop=(kt == 3))
                    inc(i, "TS")
                xp_prev_ts[0] = cnt["TS"]
                xp_ts = cnt["TS"]

                @blk.scalar
                def _(s):
                    s.wait_ge(sems["TS"], xp_ts)
                    if k >= 1:
                        s.wait_ge(sems["DS"], xdbst_ms[k - 1])
                    s.copy(out=xdb_st1[:, :], in_=PSE[:, :])
                    inc(s.copy(out=xdb_st2[:, :], in_=PSX[:, :]), "SS")
                psex_prev_ss[0] = cnt["SS"]
                ss_xdb = cnt["SS"]

                @blk.sync
                def _(sy):
                    sy.wait_ge(sems["SS"], ss_xdb)
                    inc(sy.dma_start(out=xdbc_in[0:128, g0:g0 + TC], in_=xdb_st1[:, :]), "DS", dma=True)
                    inc(sy.dma_start(out=xdbc_in[128:160, g0:g0 + TC], in_=xdb_st2[:, :]), "DS", dma=True)
                xdbst_ms[k] = cnt["DS"]

            front_st = cnt["DS"]
            front_dg = cnt["DG"]
        es_wf.close()

        es_sc = contextlib.ExitStack()
        def _scsb(name, shape, dt):
            return es_sc.enter_context(nc.sbuf_tensor(name, shape, dt))
        dtr0 = _scsb("dtr0", [128, SC], BF16)
        dtr1 = _scsb("dtr1", [128, SC], BF16)
        brep0 = _scsb("brep0", [128, NST, SC], BF16)
        brep1 = _scsb("brep1", [128, NST, SC], BF16)
        crep0 = _scsb("crep0", [128, NST, SC], BF16)
        crep1 = _scsb("crep1", [128, NST, SC], BF16)
        u_sl0 = _scsb("u_sl0", [128, 4, SC], F32)
        u_sl1 = _scsb("u_sl1", [128, 4, SC], F32)
        sr_sl0 = _scsb("sr_sl0", [128, 4, SC], F32)
        sr_sl1 = _scsb("sr_sl1", [128, 4, SC], F32)
        delta0 = _scsb("delta0", [128, SC], F32)
        delta1 = _scsb("delta1", [128, SC], F32)
        spa = _scsb("spa", [128, SC], F32)
        spb = _scsb("spb", [128, SC], F32)
        du_s = _scsb("du_s", [128, SC], BF16)
        dA0 = _scsb("dA0", [128, NST, SC], F32)
        dA1 = _scsb("dA1", [128, NST, SC], F32)
        dBu = _scsb("dBu", [128, NST, SC], BF16)
        h_s = _scsb("h_s", [128, NST, SC], F32)
        ya0 = _scsb("ya0", [128, SC], BF16)
        ya1 = _scsb("ya1", [128, SC], BF16)
        with es_sc, nc.Block() as blk:
            @blk.gpsimd
            def _(g):
                g.wait_ge(sems["DS"], front_st)
                g.wait_ge(sems["DG"], front_dg)
                inc(g.collective_compute("AllReduce", OP.add, replica_groups=[list(range(R))],
                                         ins=[xdbc_in.ap()], outs=[xdbc_out.ap()]), "CC")
            cc_ar = cnt["CC"]

            @blk.gpsimd
            def _(g):
                g.wait_ge(sems["CC"], cc_ar)
            ld_dtr = cnt["DG"]

            if DBG:
                @blk.sync
                def _(sy):
                    sy.wait_ge(sems["CC"], cc_ar)
                    inc(sy.dma_start(out=dbg["xdbc"], in_=xdbc_out.ap()), "DS", dma=True)

            vs_rep_free = [0, 0]
            vs_dA_free = [0, 0]
            ss_dA_free = [0, 0]
            vs_sp_free = 0
            ya_dma_ms = [0, 0]
            dt_ps_ss = [0]
            ya_i = [0]

            for sc in range(NSC):
                g0 = sc * SC
                rank = g0 // TOK

                scp = sc % 2
                brep = (brep0, brep1)[scp]
                crep = (crep0, crep1)[scp]
                u_sl = (u_sl0, u_sl1)[scp]
                sr_sl = (sr_sl0, sr_sl1)[scp]
                dtr_sl = (dtr0, dtr1)[scp]

                @blk.gpsimd
                def _(g):
                    if sc == 0:
                        g.wait_ge(sems["CC"], cc_ar)
                    g.wait_ge(sems["VS"], vs_rep_free[scp])
                    inc(g.dma_start(out=brep.ap(), in_=xdbc_out[128:144, g0:g0 + SC].partition_broadcast(128)),
                        "DG", dma=True)
                    inc(g.dma_start(out=crep.ap(), in_=xdbc_out[144:160, g0:g0 + SC].partition_broadcast(128)),
                        "DG", dma=True)
                    inc(g.dma_start(out=u_sl[:, :, :], in_=u_sp[:, :, g0:g0 + SC]), "DG", dma=True)
                    inc(g.dma_start(out=sr_sl[:, :, :], in_=sr_sp[:, :, g0:g0 + SC]), "DG", dma=True)
                    inc(g.dma_start(out=dtr_sl[:, :], in_=xdbc_out[0:128, g0:g0 + SC]), "DG", dma=True)
                ld_sc = cnt["DG"]

                if sc == 0 or sc == NSC // 2:
                    @blk.vector
                    def _(v):
                        v.memset(hcarry[:, :, :], 0.0)

                for ct in range(4):
                    par = (sc * 4 + ct) % 2
                    delta = (delta0, delta1)[par]
                    dA = (dA0, dA1)[par]

                    @blk.tensor
                    def _(t):
                        t.wait_ge(sems["SS"], dt_ps_ss[0])
                        t.wait_ge(sems["DG"], ld_sc)
                        inc(t.matmul(PSA[:, 0:SC], wdt_bf[:, ct * 128:(ct + 1) * 128],
                                     dtr_sl[:, :], start=True, stop=True), "TS")
                    ts_dt = cnt["TS"]

                    @blk.scalar
                    def _(s):
                        s.wait_ge(sems["TS"], ts_dt)
                        s.wait_ge(sems["VS"], vs_sp_free)
                        sedge(s, s.activation(out=spa[:, :], in_=PSA[:, 0:SC], func=AF.Abs, bias=bdt_col[:, ct:ct + 1]))
                        sedge(s, s.activation(out=spa[:, :], in_=spa[:, :], func=AF.Exp, scale=negone_col[:, 0:1]))
                        sedge(s, s.activation(out=spa[:, :], in_=spa[:, :], func=AF.Ln, bias=ones_col[:, 0:1]))
                        i = s.activation(out=spb[:, :], in_=PSA[:, 0:SC], func=AF.Relu, bias=bdt_col[:, ct:ct + 1])
                        inc(i, "SS")
                    dt_ps_ss[0] = cnt["SS"]
                    ss_sp = cnt["SS"]

                    @blk.vector
                    def _(v):
                        v.wait_ge(sems["SS"], ss_sp)
                        v.wait_ge(sems["SS"], ss_dA_free[par])
                        inc(v.tensor_add(delta[:, :], spa[:, :], spb[:, :]), "VS")
                        v.wait_ge(sems["VS"], cnt["VS"])
                    vs_delta = cnt["VS"]
                    vs_sp_free = vs_delta

                    @blk.scalar
                    def _(s):
                        s.wait_ge(sems["VS"], vs_delta)
                        s.wait_ge(sems["VS"], vs_dA_free[par])
                        for n in range(NST):
                            i = s.activation(out=dA[:, n, :], in_=delta[:, :], func=AF.Exp,
                                             scale=a_col[:, ct, n:n + 1])
                        inc(i, "SS")
                    ss_dA = cnt["SS"]
                    ss_dA_free[par] = ss_dA

                    @blk.vector
                    def _(v):
                        v.wait_ge(sems["DG"], ld_sc)
                        vedge(v, v.tensor_mul(du_s[:, :], delta[:, :], u_sl[:, ct, :]))
                        v.tensor_tensor(
                            dBu.ap(),
                            du_s.ap().rearrange("p (o t) -> p o t", o=1).broadcast_to([128, NST, SC]),
                            brep.ap(), OP.mult)
                        v.wait_ge(sems["SS"], ss_dA)
                        for n in range(NST):
                            i = v.tensor_tensor_scan(out=h_s[:, n, :], data0=dA[:, n, :], data1=dBu[:, n, :],
                                                     initial=hcarry[:, ct, n:n + 1], op0=OP.mult, op1=OP.add)
                        vedge(v, i)
                        v.tensor_copy(hcarry[:, ct, :],
                                      h_s.ap()[:, :, SC - 1:SC].rearrange("p n o -> p (n o)"))
                        v.tensor_tensor(dA.ap(), h_s.ap(), crep.ap(), OP.mult)
                        v.tensor_add(dA.ap()[:, 0:8, :].rearrange("p n t -> p (n t)"),
                                     dA.ap()[:, 0:8, :].rearrange("p n t -> p (n t)"),
                                     dA.ap()[:, 8:16, :].rearrange("p n t -> p (n t)"))
                        v.tensor_add(dA.ap()[:, 0:4, :].rearrange("p n t -> p (n t)"),
                                     dA.ap()[:, 0:4, :].rearrange("p n t -> p (n t)"),
                                     dA.ap()[:, 4:8, :].rearrange("p n t -> p (n t)"))
                        v.tensor_add(dA.ap()[:, 0:2, :].rearrange("p n t -> p (n t)"),
                                     dA.ap()[:, 0:2, :].rearrange("p n t -> p (n t)"),
                                     dA.ap()[:, 2:4, :].rearrange("p n t -> p (n t)"))
                        v.tensor_add(dA[:, 0, :], dA[:, 0, :], dA[:, 1, :])
                        ya_st = (ya0, ya1)[ya_i[0] % 2]
                        v.wait_ge(sems["DG"], ya_dma_ms[ya_i[0] % 2])
                        vedge(v, v.scalar_tensor_tensor(out=dA[:, 1, :], in0=u_sl[:, ct, :],
                                               scalar=d_col[:, ct:ct + 1], in1=dA[:, 0, :],
                                               op0=OP.mult, op1=OP.add))
                        inc(v.tensor_tensor(ya_st[:, :], dA[:, 1, :], sr_sl[:, ct, :], OP.mult), "VS")
                    vs_epi = cnt["VS"]
                    vs_dA_free[par] = vs_epi

                    @blk.gpsimd
                    def _(g):
                        g.wait_ge(sems["VS"], vs_epi)
                        ya_st = (ya0, ya1)[ya_i[0] % 2]
                        inc(g.dma_start(out=ya_in[rank * CH + ct * 128: rank * CH + (ct + 1) * 128,
                                                  (g0 % TOK):(g0 % TOK) + SC],
                                        in_=ya_st[:, :]), "DG", dma=True)
                    ya_dma_ms[ya_i[0] % 2] = cnt["DG"]
                    ya_i[0] += 1
                vs_rep_free[scp] = cnt["VS"]

            scan_st = cnt["DG"]

        with nc.Block() as blk:
            @blk.gpsimd
            def _(g):
                g.wait_ge(sems["DG"], scan_st)
                with allow_ici_all_to_all():
                    inc(g.collective_compute("AllToAll", OP.bypass, replica_groups=[list(range(R))],
                                             ins=[ya_in.ap()], outs=[ya_out.ap()]), "CC")
            cc_a2a = cnt["CC"]

            if DBG:
                @blk.sync
                def _(sy):
                    sy.wait_ge(sems["CC"], cc_a2a)
                    inc(sy.dma_start(out=dbg["ya"], in_=ya_out.ap()), "DS", dma=True)
                    inc(sy.dma_start(out=dbg["wout"], in_=wout_bf_d.ap()), "DS", dma=True)
                    sy.wait_ge(sems["CC"], cc_wag)
                    inc(sy.dma_start(out=dbg["wug"], in_=wug_out.ap()), "DS", dma=True)
                    inc(sy.dma_start(out=dbg["wdn"], in_=wdn_out.ap()), "DS", dma=True)
                    inc(sy.dma_start(out=dbg["u"], in_=u_sp.ap()), "DS", dma=True)
                    inc(sy.dma_start(out=dbg["sr"], in_=sr_sp.ap()), "DS", dma=True)

        with (
            nc.sbuf_tensor("wstream", [128, 32, 512], BF16) as wstream,
            nc.sbuf_tensor("midar", [128, 64, 512], BF16) as midar,
            nc.sbuf_tensor("stA", [128, 512], F32) as stA,
            nc.sbuf_tensor("bufB", [128, 4, D], F32) as bufB,
            nc.sbuf_tensor("bufC", [128, 4, D], F32) as bufC,
            nc.sbuf_tensor("tmB", [128, D], F32) as tmB,
            nc.sbuf_tensor("xf_fm", [128, 16, 512], BF16) as xf_fm,
            nc.sbuf_tensor("sg0", [128, 512], BF16) as sg0,
            nc.sbuf_tensor("sg1", [128, 512], BF16) as sg1,
            nc.Block() as blk,
        ):
            wo_view = wstream.ap()
            wug_v0 = wstream.ap()[:, 0:16, 0:256]
            wug_v1 = wstream.ap()[:, 16:32, 0:256]
            wdn_v0 = wstream.ap()[:, 0:32, :].rearrange("p a b -> p (a b)").rearrange(
                "p (kt q) -> p kt q", q=128)
            wdn_flat = wstream.ap().rearrange("p a b -> p (a b)")

            ya_kb0 = midar.ap()[:, 0:16, :]
            ya_kb1 = midar.ap()[:, 16:32, :]
            xf_bfview = bufC.ap().bitcast(BF16)

            dbg_stA_st = [0]
            ya_kb_ld = [0, 0]
            ya_kb_ts = [0, 0]
            wo_ld = [0]
            wo_ts = [0]
            stA_ts = [0]
            pse_ss = [0]
            psabcd_ss = [0, 0, 0, 0]
            psbf_half_ss = {0: 0, 1: 0}
            wug_ld = [0, 0]
            wug_ts = [0, 0]
            sg_vs = [0, 0]
            psug_done_v = [0, 0]
            psug_done_s = [0, 0]
            wdn_ld = [0, 0]
            wdn_ts = [0, 0]
            pscd_ss = [0, 0]
            bufB_free_vs = [0]
            bufC_free_st = [0]
            out_st_f = [0]
            midar_ts = [0]
            xf_fm_mm_ts = [0]

            def emit_stats(buf, wait_ss=None):
                @blk.vector
                def _(v):
                    if wait_ss is not None:
                        v.wait_ge(sems["SS"], wait_ss)
                    for tt in range(4):
                        i = v.reduce_sum(sums_c[:, tt:tt + 1], buf[:, tt, :], axis=AX.X)
                    inc(i, "VS")
                vs_r = cnt["VS"]

                @blk.scalar
                def _(s):
                    s.wait_ge(sems["VS"], vs_r)
                    for tt in range(4):
                        i = s.activation(out=sqs[:, :], in_=buf[:, tt, :], func=AF.Square,
                                         accum_out=ssq_c[:, tt:tt + 1])
                    inc(i, "SS")
                ss_q = cnt["SS"]

                @blk.vector
                def _(v):
                    v.wait_ge(sems["SS"], ss_q)
                    vedge(v, v.tensor_scalar(out=negmean_c[:, :], in0=sums_c[:, :], scalar1=-1.0 / D,
                                             scalar2=None, op0=OP.mult))
                    vedge(v, v.tensor_mul(t1_c[:, :], negmean_c[:, :], negmean_c[:, :]))
                    vedge(v, v.tensor_scalar(out=var_c[:, :], in0=ssq_c[:, :], scalar1=1.0 / D,
                                             scalar2=None, op0=OP.mult))
                    vedge(v, v.tensor_sub(var_c[:, :], var_c[:, :], t1_c[:, :]))
                    inc(v.tensor_scalar(out=t1_c[:, :], in0=var_c[:, :], scalar1=EPS,
                                        scalar2=None, op0=OP.add), "VS")
                vs_v = cnt["VS"]

                @blk.scalar
                def _(s):
                    s.wait_ge(sems["VS"], vs_v)
                    inc(s.activation(out=sroot_c[:, :], in_=t1_c[:, :], func=AF.Sqrt), "SS")
                ss_r = cnt["SS"]

                @blk.vector
                def _(v):
                    v.wait_ge(sems["SS"], ss_r)
                    vedge(v, v.reciprocal(stok_c[:, :], sroot_c[:, :]))
                    inc(v.tensor_mul(btok_c[:, :], negmean_c[:, :], stok_c[:, :]), "VS")
                return cnt["VS"]


            for tb in range(4):
                t0 = tb * 512

                for mtg in range(4):
                    @blk.sync
                    def _(sy):
                        if tb == 0 and mtg == 0:
                            sy.wait_ge(sems["CC"], cc_a2a)
                        sy.wait_ge(sems["TS"], wo_ts[0])
                        sy.wait_ge(sems["TS"], wdn_ts[0])
                        sy.wait_ge(sems["TS"], wdn_ts[1])
                        inc(sy.dma_start(
                            out=wo_view[:, :, :],
                            in_=wout_bf_d[:, mtg * 512:(mtg + 1) * 512].rearrange(
                                "(kt p) m -> p kt m", p=128)), "DS", dma=True)
                    wo_ld[0] = cnt["DS"]

                    for kh in range(2):
                        ya_kb = (ya_kb0, ya_kb1)[kh]

                        @blk.gpsimd
                        def _(g):
                            if tb == 0 and mtg == 0:
                                g.wait_ge(sems["CC"], cc_a2a)
                            g.wait_ge(sems["TS"], ya_kb_ts[kh])
                            g.wait_ge(sems["TS"], midar_ts[0])
                            inc(g.dma_start(
                                out=ya_kb[:, :, :],
                                in_=ya_out[kh * 2048:(kh + 1) * 2048, t0:t0 + 512].rearrange(
                                    "(kt p) t -> p kt t", p=128)), "DG", dma=True)
                        ya_kb_ld[kh] = cnt["DG"]

                        @blk.tensor
                        def _(t):
                            t.wait_ge(sems["DG"], ya_kb_ld[kh])
                            t.wait_ge(sems["DS"], wo_ld[0])
                            if kh == 0:
                                for mt in range(4):
                                    t.wait_ge(sems["SS"], psabcd_ss[mt])
                                t.wait_ge(sems["VS"], psug_done_v[0])
                                t.wait_ge(sems["VS"], psug_done_v[1])
                                t.wait_ge(sems["SS"], psug_done_s[0])
                                t.wait_ge(sems["SS"], psug_done_s[1])
                                t.wait_ge(sems["SS"], pscd_ss[0])
                                t.wait_ge(sems["SS"], pscd_ss[1])
                            for mt in range(4):
                                ps = (PSA, PSB, PSC, PSD)[mt]
                                for kt in range(16):
                                    i = t.matmul(ps[:, :], wo_view[:, kh * 16 + kt, mt * 128:(mt + 1) * 128],
                                                 ya_kb[:, kt, :],
                                                 start=(kh == 0 and kt == 0), stop=(kh == 1 and kt == 15))
                            inc(i, "TS")
                        ya_kb_ts[kh] = cnt["TS"]
                    wo_ts[0] = cnt["TS"]
                    ts_mm4 = cnt["TS"]

                    for mt in range(4):
                        gmt = mtg * 4 + mt
                        ps = (PSA, PSB, PSC, PSD)[mt]

                        @blk.scalar
                        def _(s):
                            s.wait_ge(sems["TS"], ts_mm4)
                            s.wait_ge(sems["TS"], stA_ts[0])
                            if tb >= 1 and mtg == 0 and mt == 0:
                                s.wait_ge(sems["VS"], bufB_free_vs[0])
                            if DBG:
                                s.wait_ge(sems["DS"], dbg_stA_st[0])
                            i = s.copy(out=stA[:, :], in_=ps[:, :])
                            psabcd_ss[mt] = inc(i, "SS")
                        ss_stA = cnt["SS"]

                        if DBG:
                            @blk.sync
                            def _(sy):
                                sy.wait_ge(sems["SS"], ss_stA)
                                inc(sy.dma_start(out=dbg["stA"][tb, gmt], in_=stA[:, :]), "DS", dma=True)
                            dbg_stA_st[0] = cnt["DS"]

                        @blk.tensor
                        def _(t):
                            t.wait_ge(sems["SS"], ss_stA)
                            t.wait_ge(sems["SS"], pse_ss[0])
                            for tt in range(4):
                                i = t.transpose(PSE[:, tt * 128:(tt + 1) * 128],
                                                stA[:, tt * 128:(tt + 1) * 128], identf[:, :])
                            inc(i, "TS")
                        stA_ts[0] = cnt["TS"]
                        ts_tr = cnt["TS"]

                        @blk.scalar
                        def _(s):
                            s.wait_ge(sems["TS"], ts_tr)
                            i = s.copy(out=bufB.ap()[:, :, gmt * 128:(gmt + 1) * 128],
                                       in_=PSE.ap().rearrange("p (tt q) -> p tt q", q=128))
                            inc(i, "SS")
                        pse_ss[0] = cnt["SS"]
                ss_mo = cnt["SS"]

                if DBG:
                    @blk.sync
                    def _(sy):
                        sy.wait_ge(sems["SS"], ss_mo)
                        inc(sy.dma_start(out=dbg["mo"][t0:t0 + 512, :].rearrange("(tt p) d -> p tt d", p=128),
                                         in_=bufB[:, :, :]), "DS", dma=True)
                    dbg_mo_st = cnt["DS"]

                    @blk.vector
                    def _(v):
                        v.wait_ge(sems["DS"], dbg_mo_st)

                @blk.sync
                def _(sy):
                    sy.wait_ge(sems["DS"], bufC_free_st[0])
                    inc(sy.dma_start(out=bufC[:, :, :],
                                     in_=x_my[t0:t0 + 512, :].rearrange("(tt p) d -> p tt d", p=128)),
                        "DS", dma=True)
                ld_xs = cnt["DS"]

                @blk.vector
                def _(v):
                    v.wait_ge(sems["DS"], ld_xs)
                vs_tmp = emit_stats(bufC)

                @blk.vector
                def _(v):
                    v.wait_ge(sems["SS"], ss_mo)
                    for tt in range(4):
                        v.tensor_scalar(out=tmB[:, :], in0=bufC[:, tt, :],
                                        scalar1=negmean_c[:, tt:tt + 1], scalar2=stok_c[:, tt:tt + 1],
                                        op0=OP.add, op1=OP.mult)
                        i = v.tensor_add(bufB[:, tt, :], bufB[:, tt, :], tmB[:, :])
                    inc(i, "VS")

                vs_tmp = emit_stats(bufB)

                @blk.vector
                def _(v):
                    for tt in range(4):
                        v.tensor_scalar(out=tmB[:, :], in0=bufB[:, tt, :],
                                        scalar1=negmean_c[:, tt:tt + 1], scalar2=stok_c[:, tt:tt + 1],
                                        op0=OP.add, op1=OP.mult)
                        i = v.tensor_add(bufB[:, tt, :], bufC[:, tt, :], tmB[:, :])
                    inc(i, "VS")
                vs_x1 = cnt["VS"]

                if DBG:
                    @blk.sync
                    def _(sy):
                        sy.wait_ge(sems["VS"], vs_x1)
                        inc(sy.dma_start(out=dbg["x1"][t0:t0 + 512, :].rearrange("(tt p) d -> p tt d", p=128),
                                         in_=bufB[:, :, :]), "DS", dma=True)
                    dbg_x1_st = cnt["DS"]

                    @blk.vector
                    def _(v):
                        v.wait_ge(sems["DS"], dbg_x1_st)

                vs_tmp = emit_stats(bufB)

                @blk.scalar
                def _(s):
                    s.wait_ge(sems["VS"], vs_tmp)
                    for tt in range(4):
                        i = s.activation(out=xf_bfview[:, tt, 0:D], in_=bufB[:, tt, :], func=AF.Identity,
                                         scale=stok_c[:, tt:tt + 1], bias=btok_c[:, tt:tt + 1])
                    inc(i, "SS")
                ss_xf = cnt["SS"]

                for tt in range(4):
                    for hh in range(2):
                        half = (tt * 2 + hh) % 2

                        @blk.tensor
                        def _(t):
                            if tt == 0 and hh == 0:
                                t.wait_ge(sems["SS"], ss_xf)
                                t.wait_ge(sems["TS"], xf_fm_mm_ts[0])
                            t.wait_ge(sems["SS"], psbf_half_ss[half])
                            for dt in range(hh * 8, hh * 8 + 8):
                                i = t.transpose(
                                    PSBF[:, half * 1024 + (dt - hh * 8) * 128:
                                         half * 1024 + (dt - hh * 8 + 1) * 128],
                                    xf_bfview[:, tt, dt * 128:(dt + 1) * 128], identb[:, :])
                            inc(i, "TS")
                        ts_xf = cnt["TS"]

                        @blk.scalar
                        def _(s):
                            s.wait_ge(sems["TS"], ts_xf)
                            for dt in range(hh * 8, hh * 8 + 8):
                                i = s.mul(out=xf_fm[:, dt, tt * 128:(tt + 1) * 128],
                                          in_=PSBF[:, half * 1024 + (dt - hh * 8) * 128:
                                                   half * 1024 + (dt - hh * 8 + 1) * 128],
                                          mul=suug_col[:, dt:dt + 1])
                            inc(i, "SS")
                        psbf_half_ss[half] = cnt["SS"]
                ss_xffm = cnt["SS"]

                if DBG:
                    @blk.sync
                    def _(sy):
                        sy.wait_ge(sems["SS"], ss_xffm)
                        inc(sy.dma_start(out=dbg["xf"][tb], in_=xf_fm[:, :, :]), "DS", dma=True)

                for mi in range(64):
                    wug_v = (wug_v0, wug_v1)[mi % 2]
                    rr = mi // 8
                    scol = (mi % 8) * 128

                    @blk.sync
                    def _(sy):
                        if mi < 2:
                            sy.wait_ge(sems["TS"], wo_ts[0])
                        sy.wait_ge(sems["TS"], wug_ts[mi % 2])
                        inc(sy.dma_start(out=wug_v[:, :, 0:128],
                                     in_=wug_out[rr * D:(rr + 1) * D, scol:scol + 128].rearrange(
                                         "(kt p) m -> p kt m", p=128)), "DS", dma=True)
                        inc(sy.dma_start(out=wug_v[:, :, 128:256],
                                         in_=wug_out[rr * D:(rr + 1) * D, 1024 + scol:1024 + scol + 128].rearrange(
                                             "(kt p) m -> p kt m", p=128)), "DS", dma=True)
                    wug_ld[mi % 2] = cnt["DS"]

                    psU = (PSA, PSC)[mi % 2]
                    psG = (PSB, PSD)[mi % 2]

                    @blk.tensor
                    def _(t):
                        t.wait_ge(sems["DS"], wug_ld[mi % 2])
                        if mi == 0:
                            t.wait_ge(sems["SS"], ss_xffm)
                        t.wait_ge(sems["VS"], psug_done_v[mi % 2])
                        t.wait_ge(sems["SS"], psug_done_s[mi % 2])
                        for kt in range(16):
                            t.matmul(psU[:, :], wug_v[:, kt, 0:128], xf_fm[:, kt, :],
                                     start=(kt == 0), stop=(kt == 15))
                        for kt in range(16):
                            i = t.matmul(psG[:, :], wug_v[:, kt, 128:256], xf_fm[:, kt, :],
                                         start=(kt == 0), stop=(kt == 15))
                        inc(i, "TS")
                    wug_ts[mi % 2] = cnt["TS"]
                    ts_ug = cnt["TS"]

                    sg = (sg0, sg1)[mi % 2]

                    @blk.scalar
                    def _(s):
                        s.wait_ge(sems["TS"], ts_ug)
                        s.wait_ge(sems["VS"], sg_vs[mi % 2])
                        i = s.activation(out=sg[:, :], in_=psG[:, :], func=AF.Silu)
                        inc(i, "SS")
                    psug_done_s[mi % 2] = cnt["SS"]
                    ss_sg = cnt["SS"]

                    @blk.vector
                    def _(v):
                        v.wait_ge(sems["SS"], ss_sg)
                        i = v.scalar_tensor_tensor(out=midar[:, mi, :], in0=psU[:, :],
                                                   scalar=sudp_col[:, mi:mi + 1], in1=sg[:, :],
                                                   op0=OP.mult, op1=OP.mult)
                        inc(i, "VS")
                    psug_done_v[mi % 2] = cnt["VS"]
                    sg_vs[mi % 2] = cnt["VS"]
                vs_mid = cnt["VS"]

                if DBG:
                    @blk.sync
                    def _(sy):
                        sy.wait_ge(sems["VS"], vs_mid)
                        inc(sy.dma_start(out=dbg["mid"][tb], in_=midar[:, :, :]), "DS", dma=True)
                    dbg_mid_st = cnt["DS"]

                    @blk.gpsimd
                    def _(g):
                        g.wait_ge(sems["DS"], dbg_mid_st)

                for dmt in range(16):
                    rr = dmt // 2
                    c0 = (dmt % 2) * 128
                    wdn_v = wdn_flat[:, (dmt % 2) * 8192:(dmt % 2) * 8192 + 8192].rearrange(
                        "p (kt q) -> p kt q", q=128)
                    ps = (PSC, PSD)[dmt % 2]

                    @blk.sync
                    def _(sy):
                        if dmt < 2:
                            sy.wait_ge(sems["TS"], wug_ts[0])
                            sy.wait_ge(sems["TS"], wug_ts[1])
                        sy.wait_ge(sems["TS"], wdn_ts[dmt % 2])
                        inc(sy.dma_start(out=wdn_v,
                                         in_=wdn_out[rr * INTER:(rr + 1) * INTER, c0:c0 + 128].rearrange(
                                             "(kt p) m -> p kt m", p=128)), "DS", dma=True)
                    wdn_ld[dmt % 2] = cnt["DS"]

                    @blk.tensor
                    def _(t):
                        t.wait_ge(sems["DS"], wdn_ld[dmt % 2])
                        if dmt == 0:
                            t.wait_ge(sems["VS"], vs_mid)
                            t.wait_ge(sems["VS"], psug_done_v[1])
                            t.wait_ge(sems["SS"], psug_done_s[1])
                        t.wait_ge(sems["SS"], pscd_ss[dmt % 2])
                        for kt in range(64):
                            i = t.matmul(ps[:, :], wdn_v[:, kt, :], midar[:, kt, :],
                                         start=(kt == 0), stop=(kt == 63))
                        inc(i, "TS")
                    wdn_ts[dmt % 2] = cnt["TS"]
                    ts_dn = cnt["TS"]

                    @blk.scalar
                    def _(s):
                        s.wait_ge(sems["TS"], ts_dn)
                        s.wait_ge(sems["TS"], stA_ts[0])
                        if dmt == 0:
                            s.wait_ge(sems["DS"], out_st_f[0])
                        i = s.copy(out=stA[:, :], in_=ps[:, :])
                        pscd_ss[dmt % 2] = inc(i, "SS")
                    ss_dnA = cnt["SS"]

                    @blk.tensor
                    def _(t):
                        t.wait_ge(sems["SS"], ss_dnA)
                        t.wait_ge(sems["SS"], pse_ss[0])
                        for tt in range(4):
                            i = t.transpose(PSE[:, tt * 128:(tt + 1) * 128],
                                            stA[:, tt * 128:(tt + 1) * 128], identf[:, :])
                        inc(i, "TS")
                    stA_ts[0] = cnt["TS"]
                    ts_tr2 = cnt["TS"]

                    @blk.scalar
                    def _(s):
                        s.wait_ge(sems["TS"], ts_tr2)
                        i = s.copy(out=bufC.ap()[:, :, dmt * 128:(dmt + 1) * 128],
                                   in_=PSE.ap().rearrange("p (tt q) -> p tt q", q=128))
                        inc(i, "SS")
                    pse_ss[0] = cnt["SS"]
                midar_ts[0] = cnt["TS"]
                xf_fm_mm_ts[0] = wug_ts[1] if tb >= 0 else 0
                ss_fo = cnt["SS"]

                if DBG:
                    @blk.sync
                    def _(sy):
                        sy.wait_ge(sems["SS"], ss_fo)
                        inc(sy.dma_start(out=dbg["fo"][t0:t0 + 512, :].rearrange("(tt p) d -> p tt d", p=128),
                                         in_=bufC[:, :, :]), "DS", dma=True)
                    dbg_fo_st = cnt["DS"]

                    @blk.vector
                    def _(v):
                        v.wait_ge(sems["DS"], dbg_fo_st)

                vs_tmp = emit_stats(bufC, wait_ss=ss_fo)

                @blk.vector
                def _(v):
                    v.wait_ge(sems["SS"], ss_fo)
                    for tt in range(4):
                        v.tensor_scalar(out=tmB[:, :], in0=bufC[:, tt, :],
                                        scalar1=negmean_c[:, tt:tt + 1], scalar2=stok_c[:, tt:tt + 1],
                                        op0=OP.add, op1=OP.mult)
                        i = v.tensor_add(bufC[:, tt, :], bufB[:, tt, :], tmB[:, :])
                    inc(i, "VS")
                bufB_free_vs[0] = cnt["VS"]
                vs_out = cnt["VS"]

                @blk.sync
                def _(sy):
                    sy.wait_ge(sems["VS"], vs_out)
                    inc(sy.dma_start(out=out_tok[t0:t0 + 512, :].rearrange("(tt p) d -> p tt d", p=128),
                                     in_=bufC[:, :, :]), "DS", dma=True)
                bufC_free_st[0] = cnt["DS"]
                out_st_f[0] = cnt["DS"]

            final_st = cnt["DS"]
            final_dg = cnt["DG"]

            @blk.sync
            def _(sy):
                sy.wait_ge(sems["DS"], final_st)
                sy.wait_ge(sems["DG"], final_dg)

    return nc


_NC_CACHE = {}


def kernel(**inputs):
    x = np.ascontiguousarray(np.asarray(inputs["L0_attn_x"], dtype=np.float32).reshape(T, D))
    w_norm = np.asarray(inputs["L0_attn_w_norm"], dtype=np.float32)
    W_in = np.asarray(inputs["L0_attn_W_in"], dtype=np.float32)
    conv_w = np.asarray(inputs["L0_attn_conv_w"], dtype=np.float32)
    conv_b = np.asarray(inputs["L0_attn_conv_b"], dtype=np.float32)
    W_xp = np.asarray(inputs["L0_attn_W_xp"], dtype=np.float32)
    W_dt = np.asarray(inputs["L0_attn_W_dt"], dtype=np.float32)
    b_dt = np.asarray(inputs["L0_attn_b_dt"], dtype=np.float32)
    A_log = np.asarray(inputs["L0_attn_A_log"], dtype=np.float32)
    Dv = np.asarray(inputs["L0_attn_D"], dtype=np.float32)
    W_out = np.asarray(inputs["L0_attn_W_out"], dtype=np.float32)
    SU_ug = np.asarray(inputs["L0_ffn_SU_ug"], dtype=np.float32)
    SU_dp = np.asarray(inputs["L0_ffn_SU_dp"], dtype=np.float32)
    Ws_ug = np.asarray(inputs["L0_ffn_Wscale_ug"], dtype=np.float32)
    Ws_dp = np.asarray(inputs["L0_ffn_Wscale_dp"], dtype=np.float32)
    W_up = np.asarray(inputs["L0_ffn_W_up"], dtype=np.float32)
    W_gate = np.asarray(inputs["L0_ffn_W_gate"], dtype=np.float32)
    W_down = np.asarray(inputs["L0_ffn_W_down"], dtype=np.float32)

    identb = np.eye(128, dtype=np.float32).astype(ml_dtypes.bfloat16)
    identf = np.eye(128, dtype=np.float32)

    if "nc" not in _NC_CACHE:
        _NC_CACHE["nc"] = build_nc()
    nc = _NC_CACHE["nc"]

    def colN(v, pad=None):
        n = v.shape[0] // 128
        out = np.ascontiguousarray(v.reshape(n, 128).T)
        if pad is not None and n < pad:
            out = np.concatenate([out, np.zeros((128, pad - n), np.float32)], axis=1)
        return out

    def col4(v):
        return colN(v)

    in_maps = []
    for c in range(R):
        ch0 = c * CH
        im = {
            "x_full": x,
            "x_my": np.ascontiguousarray(x[c * TOK:(c + 1) * TOK]),
            "w_in": np.ascontiguousarray(
                np.concatenate([W_in[:, ch0:ch0 + CH], W_in[:, DI + ch0:DI + ch0 + CH]], axis=1)),
            "conv_w": np.ascontiguousarray(
                conv_w[:, ch0:ch0 + CH].reshape(4, 4, 128).transpose(2, 1, 0)),
            "conv_b": col4(conv_b[ch0:ch0 + CH]),
            "w_xp": np.ascontiguousarray(W_xp[ch0:ch0 + CH]),
            "w_dt": np.ascontiguousarray(W_dt[:, ch0:ch0 + CH]),
            "b_dt": col4(b_dt[ch0:ch0 + CH]),
            "a_log": np.ascontiguousarray(A_log[ch0:ch0 + CH].reshape(4, 128, NST).transpose(1, 0, 2)),
            "d_vec": col4(Dv[ch0:ch0 + CH]),
            "w_norm": colN(w_norm),
            "w_out": W_out,
            "w_up_sh": np.ascontiguousarray(W_up[c * (INTER // R):(c + 1) * (INTER // R)]),
            "w_gate_sh": np.ascontiguousarray(W_gate[c * (INTER // R):(c + 1) * (INTER // R)]),
            "w_down_sh": np.ascontiguousarray(W_down[c * (D // R):(c + 1) * (D // R)]),
            "ws_ug_sh": colN(Ws_ug[c * (INTER // R):(c + 1) * (INTER // R)]),
            "ws_gt_sh": colN(Ws_ug[INTER + c * (INTER // R):INTER + (c + 1) * (INTER // R)], pad=8),
            "ws_dp_sh": colN(Ws_dp[c * (D // R):(c + 1) * (D // R)], pad=8),
            "su_ug": colN(SU_ug),
            "su_dp": colN(SU_dp),
            "identb": identb,
            "identf": identf,
        }
        in_maps.append(im)

    res = run_bass_kernel_spmd(nc, in_maps, core_ids=list(range(R)))
    out = np.concatenate([res.results[c]["out_tok"] for c in range(R)], axis=0)
    if DBG:
        kernel.last_results = res
    return out.reshape(2, 8192, 2048)


# version 05
# speedup vs baseline: 1.0343x; 1.0343x over previous
import contextlib

import numpy as np
import ml_dtypes

import concourse.bass as bass
import concourse.mybir as mybir
from concourse.bass_utils import run_bass_kernel_spmd
from concourse.aot_env import allow_ici_all_to_all

F32 = mybir.dt.float32
BF16 = mybir.dt.bfloat16
AF = mybir.ActivationFunctionType
OP = mybir.AluOpType
AX = mybir.AxisListType

R = 8
D = 2048
DI, NST, DTR = 4096, 16, 128
T = 16384
CH = DI // R
TOK = T // R
TC = 512
NCHUNK = T // TC
SC = 256
NSC = T // SC
INTER = 8192
EPS = 1e-5
RSD = float(1.0 / np.sqrt(D))
RSI = float(1.0 / np.sqrt(INTER))

DBG = False


def build_nc():
    nc = bass.Bass("TRN2", target_bir_lowering=False)

    x_full = nc.dram_tensor("x_full", [T, D], F32, kind="ExternalInput").ap()
    x_my = nc.dram_tensor("x_my", [TOK, D], F32, kind="ExternalInput").ap()
    w_in = nc.dram_tensor("w_in", [D, 2 * CH], F32, kind="ExternalInput").ap()
    conv_w = nc.dram_tensor("conv_w", [128, 4, 4], F32, kind="ExternalInput").ap()
    conv_b = nc.dram_tensor("conv_b", [128, 4], F32, kind="ExternalInput").ap()
    w_xp = nc.dram_tensor("w_xp", [CH, 160], F32, kind="ExternalInput").ap()
    w_dt = nc.dram_tensor("w_dt", [DTR, CH], F32, kind="ExternalInput").ap()
    b_dt = nc.dram_tensor("b_dt", [128, 4], F32, kind="ExternalInput").ap()
    a_log = nc.dram_tensor("a_log", [128, 4, NST], F32, kind="ExternalInput").ap()
    d_vec = nc.dram_tensor("d_vec", [128, 4], F32, kind="ExternalInput").ap()
    w_norm = nc.dram_tensor("w_norm", [128, 16], F32, kind="ExternalInput").ap()
    w_out = nc.dram_tensor("w_out", [DI, D], F32, kind="ExternalInput").ap()
    w_up_sh = nc.dram_tensor("w_up_sh", [INTER // R, D], F32, kind="ExternalInput").ap()
    w_gate_sh = nc.dram_tensor("w_gate_sh", [INTER // R, D], F32, kind="ExternalInput").ap()
    w_down_sh = nc.dram_tensor("w_down_sh", [D // R, INTER], F32, kind="ExternalInput").ap()
    ws_ug_sh = nc.dram_tensor("ws_ug_sh", [128, 8], F32, kind="ExternalInput").ap()
    ws_gt_sh = nc.dram_tensor("ws_gt_sh", [128, 8], F32, kind="ExternalInput").ap()
    ws_dp_sh = nc.dram_tensor("ws_dp_sh", [128, 8], F32, kind="ExternalInput").ap()
    su_ug = nc.dram_tensor("su_ug", [128, 16], F32, kind="ExternalInput").ap()
    su_dp = nc.dram_tensor("su_dp", [128, 64], F32, kind="ExternalInput").ap()
    identb_d = nc.dram_tensor("identb", [128, 128], BF16, kind="ExternalInput").ap()
    identf_d = nc.dram_tensor("identf", [128, 128], F32, kind="ExternalInput").ap()

    out_tok = nc.dram_tensor("out_tok", [TOK, D], F32, kind="ExternalOutput").ap()

    xdbc_in = nc.dram_tensor("xdbc_in", [160, T], F32)
    xdbc_out = nc.dram_tensor("xdbc_out", [160, T], F32, addr_space="Shared")
    u_sp = nc.dram_tensor("u_sp", [128, 4, T], F32)
    sr_sp = nc.dram_tensor("sr_sp", [128, 4, T], F32)
    ya_in = nc.dram_tensor("ya_in", [R * CH, TOK], BF16)
    ya_out = nc.dram_tensor("ya_out", [R * CH, TOK], BF16)
    wug_in = nc.dram_tensor("wug_in", [D, 2 * INTER // R], BF16)
    wug_out = nc.dram_tensor("wug_out", [R * D, 2 * INTER // R], BF16, addr_space="Shared")
    wdn_in = nc.dram_tensor("wdn_in", [INTER, D // R], BF16)
    wdn_out = nc.dram_tensor("wdn_out", [R * INTER, D // R], BF16, addr_space="Shared")
    wout_bf_d = nc.dram_tensor("wout_bf_d", [DI, D], BF16)

    dbg = {}
    if DBG:
        dbg["u"] = nc.dram_tensor("dbg_u", [128, 4, T], F32, kind="ExternalOutput").ap()
        dbg["sr"] = nc.dram_tensor("dbg_sr", [128, 4, T], F32, kind="ExternalOutput").ap()
        dbg["xdbc"] = nc.dram_tensor("dbg_xdbc", [160, T], F32, kind="ExternalOutput").ap()
        dbg["ya"] = nc.dram_tensor("dbg_ya", [R * CH, TOK], BF16, kind="ExternalOutput").ap()
        dbg["x1"] = nc.dram_tensor("dbg_x1", [TOK, D], F32, kind="ExternalOutput").ap()
        dbg["mo"] = nc.dram_tensor("dbg_mo", [TOK, D], F32, kind="ExternalOutput").ap()
        dbg["wout"] = nc.dram_tensor("dbg_wout", [DI, D], BF16, kind="ExternalOutput").ap()
        dbg["stA"] = nc.dram_tensor("dbg_stA", [4, 16, 128, 512], F32, kind="ExternalOutput").ap()
        dbg["wug"] = nc.dram_tensor("dbg_wug", [R * D, 2 * INTER // R], BF16, kind="ExternalOutput").ap()
        dbg["wdn"] = nc.dram_tensor("dbg_wdn", [R * INTER, D // R], BF16, kind="ExternalOutput").ap()
        dbg["xf"] = nc.dram_tensor("dbg_xf", [4, 128, 16, 512], BF16, kind="ExternalOutput").ap()
        dbg["mid"] = nc.dram_tensor("dbg_mid", [4, 128, 64, 512], BF16, kind="ExternalOutput").ap()
        dbg["fo"] = nc.dram_tensor("dbg_fo", [TOK, D], F32, kind="ExternalOutput").ap()

    es = contextlib.ExitStack()

    def sb(name, shape, dt):
        return es.enter_context(nc.sbuf_tensor(name, shape, dt))

    def pt(name, shape, dt):
        return es.enter_context(nc.psum_tensor(name, shape, dt))

    cnt = {}
    sems = {}

    with es:
        for s_ in ["DS", "DG", "VS", "SS", "TS", "GS", "CC", "VE", "SE"]:
            sems[s_] = es.enter_context(nc.semaphore(s_))
            cnt[s_] = 0

        def vedge(v, instr):
            cnt["VE"] += 1
            instr.then_inc(sems["VE"], 1)
            v.wait_ge(sems["VE"], cnt["VE"])

        def sedge(se, instr):
            cnt["SE"] += 1
            instr.then_inc(sems["SE"], 1)
            se.wait_ge(sems["SE"], cnt["SE"])

        def inc(instr, name, dma=False):
            if name == "CC":
                instr.then_inc(sems["CC"])
                cnt["CC"] += 1
            else:
                v = 16 if dma else 1
                instr.then_inc(sems[name], v)
                cnt[name] += v
            return cnt[name]

        identb = sb("identb_s", [128, 128], BF16)
        identf = sb("identf_s", [128, 128], F32)
        wxp_bf = sb("wxp_bf", [128, 4, 160], F32)
        wdt_bf = sb("wdt_bf", [128, CH], BF16)
        a_col = sb("a_col", [128, 4, NST], F32)
        d_col = sb("d_col", [128, 4], F32)
        cb_col = sb("cb_col", [128, 4], F32)
        cw_col = sb("cw_col", [128, 4, 4], F32)
        bdt_col = sb("bdt_col", [128, 4], F32)
        suug_col = sb("suug_col", [128, 16], F32)
        sudp_col = sb("sudp_col", [128, 64], F32)
        ones_col = sb("ones_col", [128, 1], F32)
        negone_col = sb("negone_col", [128, 1], F32)
        hcarry = sb("hcarry", [128, 4, NST], F32)
        wn_c = sb("wn_c", [128, 16], F32)
        sqs = sb("sqs", [128, D], F32)
        sums_c = sb("sums_c", [128, 4], F32)
        ssq_c = sb("ssq_c", [128, 4], F32)
        negmean_c = sb("negmean_c", [128, 4], F32)
        var_c = sb("var_c", [128, 4], F32)
        t1_c = sb("t1_c", [128, 4], F32)
        t2_c = sb("t2_c", [128, 4], F32)
        sroot_c = sb("sroot_c", [128, 4], F32)
        stok_c = sb("stok_c", [128, 4], F32)
        btok_c = sb("btok_c", [128, 4], F32)

        PSA = pt("PSA", [128, 512], F32)
        PSB = pt("PSB", [128, 512], F32)
        PSC = pt("PSC", [128, 512], F32)
        PSD = pt("PSD", [128, 512], F32)
        PSE = pt("PSE", [128, 512], F32)
        PSX = pt("PSX", [32, 512], F32)
        PSBF = pt("PSBF", [128, 2048], BF16)

        es_wf = contextlib.ExitStack()
        win_bf = es_wf.enter_context(nc.sbuf_tensor("win_bf", [128, 16, 2 * CH], BF16))
        with (
            nc.sbuf_tensor("wt_a", [128, 8, D], F32) as wt_a,
            nc.sbuf_tensor("wt_b", [128, 8, D], F32) as wt_b,
            nc.sbuf_tensor("wt_bf", [128, 8, D], BF16) as wt_bf,
            nc.sbuf_tensor("wt_st", [128, 1024], BF16) as wt_st,
            nc.sbuf_tensor("wsc_c", [128, 8], F32) as wsc_c,
            nc.Block() as blk,
        ):
            @blk.sync
            def _(sy):
                inc(sy.dma_start(out=identb[:, :], in_=identb_d), "DS", dma=True)
                inc(sy.dma_start(out=identf[:, :], in_=identf_d), "DS", dma=True)
                inc(sy.dma_start(out=cb_col[:, :], in_=conv_b), "DS", dma=True)
                inc(sy.dma_start(out=cw_col[:, :, :], in_=conv_w), "DS", dma=True)
                inc(sy.dma_start(out=d_col[:, :], in_=d_vec), "DS", dma=True)
                inc(sy.dma_start(out=bdt_col[:, :], in_=b_dt), "DS", dma=True)
                inc(sy.dma_start(out=suug_col[:, :], in_=su_ug), "DS", dma=True)
                inc(sy.dma_start(out=sudp_col[:, :], in_=su_dp), "DS", dma=True)
                inc(sy.dma_start(out=a_col[:, :, :], in_=a_log), "DS", dma=True)
                inc(sy.dma_start(out=wn_c[:, :], in_=w_norm), "DS", dma=True)

            @blk.gpsimd
            def _(g):
                inc(g.memset(ones_col[:, :], 1.0), "GS")
                inc(g.memset(negone_col[:, :], -1.0), "GS")
                inc(g.dma_start(out=wxp_bf[:, :, :], in_=w_xp.rearrange("(kt p) m -> p kt m", p=128)), "DG", dma=True)
                inc(g.dma_start(out=wdt_bf[:, :], in_=w_dt), "DG", dma=True)
                inc(g.dma_start(out=wout_bf_d.ap(), in_=w_out), "DG", dma=True)

            ld_all_ds = cnt["DS"]
            ld_all_dg = cnt["DG"]
            gs_all = cnt["GS"]

            @blk.scalar
            def _(s):
                s.wait_ge(sems["DS"], ld_all_ds)
                s.wait_ge(sems["DG"], ld_all_dg)
                s.wait_ge(sems["GS"], gs_all)
                sedge(s, s.activation(out=a_col[:, :, :], in_=a_col[:, :, :], func=AF.Exp))
                inc(s.mul(out=a_col[:, :, :], in_=a_col[:, :, :], mul=negone_col[:, :]), "SS")

            prev_scale_vs = [0]
            evac_half_ss = {0: 0, 1: 0}
            st_prev_wt = [0]

            def transform(src_ap, wsc_ap, nrt, nfree, dst_dram, col0, ncols, rsn):
                wv_a = wt_a.ap().rearrange("p a b -> p (a b)").rearrange("p (rt m) -> p rt m", rt=nrt)
                wv_b = wt_b.ap().rearrange("p a b -> p (a b)").rearrange("p (rt m) -> p rt m", rt=nrt)
                wv_bf = wt_bf.ap().rearrange("p a b -> p (a b)").rearrange("p (rt m) -> p rt m", rt=nrt)

                @blk.sync
                def _(sy):
                    sy.wait_ge(sems["VS"], prev_scale_vs[0])
                    inc(sy.dma_start(out=wv_a,
                                     in_=src_ap.rearrange("(rt p) m -> p rt m", p=128)), "DS", dma=True)
                    inc(sy.dma_start(out=wsc_c[:, 0:nrt],
                                     in_=wsc_ap[:, 0:nrt]), "DS", dma=True)
                ld_here = cnt["DS"]

                @blk.vector
                def _(v):
                    v.wait_ge(sems["DS"], ld_here)
                    cur, oth = wv_a, wv_b
                    h = 1
                    while h < nfree:
                        va = cur.rearrange(
                            "p rt (nb two h) -> p rt nb two h", two=2, h=h)
                        vo = oth.rearrange(
                            "p rt (nb two h) -> p rt nb two h", two=2, h=h)
                        v.tensor_tensor(vo[:, :, :, 0, :], va[:, :, :, 0, :], va[:, :, :, 1, :], OP.add)
                        v.tensor_tensor(vo[:, :, :, 1, :], va[:, :, :, 0, :], va[:, :, :, 1, :], OP.subtract)
                        cur, oth = oth, cur
                        h *= 2
                    for rt in range(nrt):
                        i = v.tensor_scalar(out=wv_bf[:, rt, :], in0=cur[:, rt, :],
                                            scalar1=wsc_c[:, rt:rt + 1], scalar2=rsn,
                                            op0=OP.mult, op1=OP.mult)
                    inc(i, "VS")
                vs_here = cnt["VS"]
                prev_scale_vs[0] = vs_here

                ndt = nfree // 128
                for dt in range(ndt):
                    half = dt % 2

                    @blk.tensor
                    def _(t):
                        if dt == 0:
                            t.wait_ge(sems["VS"], vs_here)
                        t.wait_ge(sems["SS"], evac_half_ss[half])
                        for rt in range(nrt):
                            i = t.transpose(
                                PSBF[:, half * 1024 + rt * 128: half * 1024 + (rt + 1) * 128],
                                wv_bf[:, rt, dt * 128:(dt + 1) * 128], identb[:, :])
                        inc(i, "TS")
                    ts_here = cnt["TS"]

                    @blk.scalar
                    def _(s):
                        s.wait_ge(sems["TS"], ts_here)
                        s.wait_ge(sems["DS"], st_prev_wt[0])
                        inc(s.copy(out=wt_st[:, 0:nrt * 128],
                                   in_=PSBF[:, half * 1024:half * 1024 + nrt * 128]), "SS")
                    evac_half_ss[half] = cnt["SS"]
                    ss_here = cnt["SS"]

                    @blk.sync
                    def _(sy):
                        sy.wait_ge(sems["SS"], ss_here)
                        inc(sy.dma_start(out=dst_dram[dt * 128:(dt + 1) * 128, col0:col0 + ncols],
                                         in_=wt_st[:, 0:ncols]), "DS", dma=True)
                    st_prev_wt[0] = cnt["DS"]

            transform(w_up_sh, ws_ug_sh, 8, D, wug_in, 0, 1024, RSD)
            transform(w_gate_sh, ws_gt_sh, 8, D, wug_in, 1024, 1024, RSD)
            transform(w_down_sh, ws_dp_sh, 2, INTER, wdn_in, 0, 256, RSI)
            st_wtr = cnt["DS"]

            @blk.gpsimd
            def _(g):
                g.wait_ge(sems["DS"], st_wtr)
                inc(g.collective_compute("AllGather", OP.bypass, replica_groups=[list(range(R))],
                                         ins=[wug_in.ap()], outs=[wug_out.ap()]), "CC")
                inc(g.collective_compute("AllGather", OP.bypass, replica_groups=[list(range(R))],
                                         ins=[wdn_in.ap()], outs=[wdn_out.ap()]), "CC")
            cc_wag = cnt["CC"]

            @blk.sync
            def _(sy):
                sy.wait_ge(sems["VS"], prev_scale_vs[0])
                inc(sy.dma_start(out=wt_a.ap().rearrange("p rt (a m) -> p (rt a) m", a=2),
                                 in_=w_in.rearrange("(kt p) m -> p kt m", p=128)), "DS", dma=True)
            ld_win = cnt["DS"]

            @blk.vector
            def _(v):
                v.wait_ge(sems["DS"], ld_win)
                wv = wt_a.ap().rearrange("p rt (a m) -> p (rt a) m", a=2)
                for kt in range(16):
                    i = v.tensor_scalar(out=win_bf[:, kt, :], in0=wv[:, kt, :],
                                        scalar1=wn_c[:, kt:kt + 1], scalar2=None, op0=OP.mult)
                inc(i, "VS")
            vs_win = cnt["VS"]

        with (
            nc.sbuf_tensor("xtm0", [128, 4, D], F32) as xtm0,
            nc.sbuf_tensor("xtm1", [128, 4, D], F32) as xtm1,
            nc.sbuf_tensor("xn_tm", [128, 4, D], BF16) as xn_tm,
            nc.sbuf_tensor("xn_fm", [128, 16, TC], BF16) as xn_fm,
            nc.sbuf_tensor("xz0", [128, 4, TC + 3], F32) as xz0,
            nc.sbuf_tensor("xz1", [128, 4, TC + 3], F32) as xz1,
            nc.sbuf_tensor("convt", [128, 4, TC], F32) as convt,
            nc.sbuf_tensor("u_st", [128, 4, TC], F32) as u_st,
            nc.sbuf_tensor("sr_st", [128, 4, TC], F32) as sr_st,
            nc.sbuf_tensor("xdb_st1", [128, TC], F32) as xdb_st1,
            nc.sbuf_tensor("xdb_st2", [32, TC], F32) as xdb_st2,
            nc.Block() as blk,
        ):
            red_ms = [0] * NCHUNK
            apply_ms = [0] * NCHUNK
            transp_ms = [0] * NCHUNK
            half_evac_ss = {0: 0, 1: 0}
            evacfm_ms = [0] * NCHUNK
            psab_evac = {0: 0, 1: 0}
            conv_ms = [[0] * 4 for _ in range(NCHUNK)]
            usilu_ms = [[0] * 4 for _ in range(NCHUNK)]
            uspill_ms = [0] * NCHUNK
            srspill_ms = [0] * NCHUNK
            xdbst_ms = [0] * NCHUNK
            xz_ready_ms = [[0] * 4 for _ in range(NCHUNK)]
            tail_ms = [0] * NCHUNK
            xp_prev_ts = [0]
            psex_prev_ss = [0]

            for k in range(NCHUNK):
                xtm = (xtm0, xtm1)[k % 2]
                xz = (xz0, xz1)[k % 2]
                xzp = (xz0, xz1)[(k + 1) % 2]
                g0 = k * TC

                @blk.gpsimd
                def _(g):
                    if k >= 2:
                        g.wait_ge(sems["SS"], apply_ms[k - 2])
                        g.wait_ge(sems["VS"], red_ms[k - 2])
                    inc(g.dma_start(out=xtm[:, :, :],
                                    in_=x_full[g0:g0 + TC, :].rearrange("(tt p) d -> p tt d", p=128)),
                        "DG", dma=True)
                ld_x = cnt["DG"]

                @blk.vector
                def _(v):
                    v.wait_ge(sems["DG"], ld_x)
                    if k == 0:
                        v.wait_ge(sems["VS"], vs_win)
                    for tt in range(4):
                        i = v.reduce_sum(sums_c[:, tt:tt + 1], xtm[:, tt, :], axis=AX.X)
                    red_ms[k] = inc(i, "VS")

                @blk.scalar
                def _(s):
                    s.wait_ge(sems["DG"], ld_x)
                    for tt in range(4):
                        i = s.activation(out=sqs[:, :], in_=xtm[:, tt, :], func=AF.Square,
                                         accum_out=ssq_c[:, tt:tt + 1])
                    inc(i, "SS")
                ss_sq = cnt["SS"]

                @blk.vector
                def _(v):
                    v.wait_ge(sems["SS"], ss_sq)
                    vedge(v, v.tensor_scalar(out=negmean_c[:, :], in0=sums_c[:, :], scalar1=-1.0 / D,
                                             scalar2=None, op0=OP.mult))
                    vedge(v, v.tensor_mul(t1_c[:, :], negmean_c[:, :], negmean_c[:, :]))
                    vedge(v, v.tensor_scalar(out=var_c[:, :], in0=ssq_c[:, :], scalar1=1.0 / D,
                                             scalar2=None, op0=OP.mult))
                    vedge(v, v.tensor_sub(var_c[:, :], var_c[:, :], t1_c[:, :]))
                    vedge(v, v.tensor_scalar(out=t1_c[:, :], in0=var_c[:, :], scalar1=EPS, scalar2=None, op0=OP.add))
                    vedge(v, v.reciprocal(t2_c[:, :], t1_c[:, :]))
                    vedge(v, v.tensor_mul(t2_c[:, :], var_c[:, :], t2_c[:, :]))
                    vedge(v, v.tensor_scalar(out=t2_c[:, :], in0=t2_c[:, :], scalar1=EPS, scalar2=None, op0=OP.add))
                    inc(v.tensor_mul(t1_c[:, :], t1_c[:, :], t2_c[:, :]), "VS")
                vs_d = cnt["VS"]

                @blk.scalar
                def _(s):
                    s.wait_ge(sems["VS"], vs_d)
                    inc(s.activation(out=sroot_c[:, :], in_=t1_c[:, :], func=AF.Sqrt), "SS")
                ss_rt = cnt["SS"]

                @blk.vector
                def _(v):
                    v.wait_ge(sems["SS"], ss_rt)
                    if k >= 1:
                        v.wait_ge(sems["SS"], apply_ms[k - 1])
                    vedge(v, v.reciprocal(stok_c[:, :], sroot_c[:, :]))
                    inc(v.tensor_mul(btok_c[:, :], negmean_c[:, :], stok_c[:, :]), "VS")
                vs_tok = cnt["VS"]

                @blk.scalar
                def _(s):
                    s.wait_ge(sems["VS"], vs_tok)
                    if k >= 1:
                        s.wait_ge(sems["TS"], transp_ms[k - 1])
                    for tt in range(4):
                        i = s.activation(out=xn_tm[:, tt, :], in_=xtm[:, tt, :], func=AF.Identity,
                                         scale=stok_c[:, tt:tt + 1], bias=btok_c[:, tt:tt + 1])
                    apply_ms[k] = inc(i, "SS")

                for tt in range(4):
                    for hh in range(2):
                        half = (tt * 2 + hh) % 2

                        @blk.tensor
                        def _(t):
                            if tt == 0 and hh == 0:
                                t.wait_ge(sems["SS"], apply_ms[k])
                            t.wait_ge(sems["SS"], half_evac_ss[half])
                            for dt in range(hh * 8, hh * 8 + 8):
                                i = t.transpose(
                                    PSBF[:, half * 1024 + (dt - hh * 8) * 128:
                                         half * 1024 + (dt - hh * 8 + 1) * 128],
                                    xn_tm[:, tt, dt * 128:(dt + 1) * 128], identb[:, :])
                            inc(i, "TS")
                        ts_here = cnt["TS"]

                        @blk.scalar
                        def _(s):
                            s.wait_ge(sems["TS"], ts_here)
                            inc(s.copy(out=xn_fm[:, hh * 8:hh * 8 + 8, tt * 128:(tt + 1) * 128],
                                       in_=PSBF.ap()[:, half * 1024:half * 1024 + 1024].rearrange(
                                           "p (dt q) -> p dt q", q=128)), "SS")
                        half_evac_ss[half] = cnt["SS"]
                transp_ms[k] = cnt["TS"]
                evacfm_ms[k] = cnt["SS"]

                for mt in range(8):
                    ps = (PSA, PSB)[mt % 2]

                    @blk.tensor
                    def _(t):
                        if mt == 0:
                            t.wait_ge(sems["SS"], evacfm_ms[k])
                        t.wait_ge(sems["SS"], psab_evac[mt % 2])
                        for kt in range(16):
                            i = t.matmul(ps[:, :], win_bf[:, kt, mt * 128:(mt + 1) * 128],
                                         xn_fm[:, kt, :], start=(kt == 0), stop=(kt == 15))
                        inc(i, "TS")
                    ts_mm = cnt["TS"]

                    @blk.scalar
                    def _(s):
                        s.wait_ge(sems["TS"], ts_mm)
                        if mt < 4:
                            if k >= 2:
                                s.wait_ge(sems["VS"], conv_ms[k - 2][mt])
                            i = s.copy(out=xz[:, mt, 3:TC + 3], in_=ps[:, :])
                            xz_ready_ms[k][mt] = inc(i, "SS")
                        else:
                            ct = mt - 4
                            if k >= 1:
                                s.wait_ge(sems["DG"], srspill_ms[k - 1])
                            i = s.activation(out=sr_st[:, ct, :], in_=ps[:, :], func=AF.Silu)
                            inc(i, "SS")
                    psab_evac[mt % 2] = cnt["SS"]
                ss_sr = cnt["SS"]

                if k % (NCHUNK // 2) == 0:
                    @blk.gpsimd
                    def _(g):
                        if k >= 2:
                            g.wait_ge(sems["VS"], conv_ms[k - 2][3])
                        if k >= 1:
                            g.wait_ge(sems["SS"], tail_ms[k - 1])
                        inc(g.memset(xz[:, :, 0:3], 0.0), "GS")
                    gs_tail = cnt["GS"]
                    tail_is_gs = True
                else:
                    @blk.scalar
                    def _(s):
                        i = s.copy(out=xz[:, :, 0:3], in_=xzp[:, :, TC:TC + 3])
                        tail_ms[k] = inc(i, "SS")
                    gs_tail = None
                    tail_is_gs = False

                for ct in range(4):
                    @blk.vector
                    def _(v):
                        v.wait_ge(sems["SS"], xz_ready_ms[k][ct])
                        if tail_is_gs:
                            v.wait_ge(sems["GS"], gs_tail)
                        else:
                            v.wait_ge(sems["SS"], tail_ms[k])
                        if k >= 1:
                            v.wait_ge(sems["SS"], usilu_ms[k - 1][ct])
                        v.tensor_scalar(out=convt[:, ct, :], in0=xz[:, ct, 0:TC],
                                        scalar1=cw_col[:, ct, 0:1], scalar2=None, op0=OP.mult)
                        for kk in range(1, 4):
                            i = v.scalar_tensor_tensor(out=convt[:, ct, :], in0=xz[:, ct, kk:kk + TC],
                                                       scalar=cw_col[:, ct, kk:kk + 1],
                                                       in1=convt[:, ct, :], op0=OP.mult, op1=OP.add)
                        conv_ms[k][ct] = inc(i, "VS")

                    @blk.scalar
                    def _(s):
                        s.wait_ge(sems["VS"], conv_ms[k][ct])
                        if k >= 1:
                            s.wait_ge(sems["DG"], uspill_ms[k - 1])
                            s.wait_ge(sems["TS"], xp_prev_ts[0])
                        i = s.activation(out=u_st[:, ct, :], in_=convt[:, ct, :], func=AF.Silu,
                                         bias=cb_col[:, ct:ct + 1])
                        usilu_ms[k][ct] = inc(i, "SS")
                ss_u = cnt["SS"]

                @blk.gpsimd
                def _(g):
                    g.wait_ge(sems["SS"], ss_u)
                    inc(g.dma_start(out=u_sp[:, :, g0:g0 + TC], in_=u_st[:, :, :]), "DG", dma=True)
                    uspill_ms[k] = cnt["DG"]
                    g.wait_ge(sems["SS"], ss_sr)
                    inc(g.dma_start(out=sr_sp[:, :, g0:g0 + TC], in_=sr_st[:, :, :]), "DG", dma=True)
                    srspill_ms[k] = cnt["DG"]

                @blk.tensor
                def _(t):
                    t.wait_ge(sems["SS"], ss_u)
                    t.wait_ge(sems["SS"], psex_prev_ss[0])
                    for kt in range(4):
                        t.matmul(PSE[:, :], wxp_bf[:, kt, 0:128], u_st[:, kt, :],
                                 start=(kt == 0), stop=(kt == 3))
                    for kt in range(4):
                        i = t.matmul(PSX[:, :], wxp_bf[:, kt, 128:160], u_st[:, kt, :],
                                     start=(kt == 0), stop=(kt == 3))
                    inc(i, "TS")
                xp_prev_ts[0] = cnt["TS"]
                xp_ts = cnt["TS"]

                @blk.scalar
                def _(s):
                    s.wait_ge(sems["TS"], xp_ts)
                    if k >= 1:
                        s.wait_ge(sems["DS"], xdbst_ms[k - 1])
                    s.copy(out=xdb_st1[:, :], in_=PSE[:, :])
                    inc(s.copy(out=xdb_st2[:, :], in_=PSX[:, :]), "SS")
                psex_prev_ss[0] = cnt["SS"]
                ss_xdb = cnt["SS"]

                @blk.sync
                def _(sy):
                    sy.wait_ge(sems["SS"], ss_xdb)
                    inc(sy.dma_start(out=xdbc_in[0:128, g0:g0 + TC], in_=xdb_st1[:, :]), "DS", dma=True)
                    inc(sy.dma_start(out=xdbc_in[128:160, g0:g0 + TC], in_=xdb_st2[:, :]), "DS", dma=True)
                xdbst_ms[k] = cnt["DS"]

            front_st = cnt["DS"]
            front_dg = cnt["DG"]
        es_wf.close()

        es_sc = contextlib.ExitStack()
        def _scsb(name, shape, dt):
            return es_sc.enter_context(nc.sbuf_tensor(name, shape, dt))
        dtr0 = _scsb("dtr0", [128, SC], BF16)
        dtr1 = _scsb("dtr1", [128, SC], BF16)
        brep0 = _scsb("brep0", [128, NST, SC], F32)
        brep1 = _scsb("brep1", [128, NST, SC], F32)
        crep0 = _scsb("crep0", [128, NST, SC], F32)
        crep1 = _scsb("crep1", [128, NST, SC], F32)
        u_sl0 = _scsb("u_sl0", [128, 4, SC], F32)
        u_sl1 = _scsb("u_sl1", [128, 4, SC], F32)
        sr_sl0 = _scsb("sr_sl0", [128, 4, SC], F32)
        sr_sl1 = _scsb("sr_sl1", [128, 4, SC], F32)
        delta0 = _scsb("delta0", [128, SC], F32)
        delta1 = _scsb("delta1", [128, SC], F32)
        delta2 = _scsb("delta2", [128, SC], F32)
        delta3 = _scsb("delta3", [128, SC], F32)
        spa = _scsb("spa", [128, SC], F32)
        spb = _scsb("spb", [128, SC], F32)
        du_s = _scsb("du_s", [128, SC], F32)
        dA0 = _scsb("dA0", [128, NST, SC], F32)
        dA1 = _scsb("dA1", [128, NST, SC], F32)
        dBu = _scsb("dBu", [128, NST, SC], F32)
        h_s = _scsb("h_s", [128, NST, SC], F32)
        ya0 = _scsb("ya0", [128, SC], BF16)
        ya1 = _scsb("ya1", [128, SC], BF16)
        with es_sc, nc.Block() as blk:
            @blk.gpsimd
            def _(g):
                g.wait_ge(sems["DS"], front_st)
                g.wait_ge(sems["DG"], front_dg)
                inc(g.collective_compute("AllReduce", OP.add, replica_groups=[list(range(R))],
                                         ins=[xdbc_in.ap()], outs=[xdbc_out.ap()]), "CC")
            cc_ar = cnt["CC"]

            @blk.gpsimd
            def _(g):
                g.wait_ge(sems["CC"], cc_ar)
            ld_dtr = cnt["DG"]

            if DBG:
                @blk.sync
                def _(sy):
                    sy.wait_ge(sems["CC"], cc_ar)
                    inc(sy.dma_start(out=dbg["xdbc"], in_=xdbc_out.ap()), "DS", dma=True)

            vs_rep_free = [0, 0]
            vs_dA_free = [0, 0]
            ss_dA_ct = [0, 0, 0, 0]
            vs_delta_ct = [0, 0, 0, 0]
            vs_sp_free = 0
            ya_dma_ms = [0, 0]
            dt_ps_ss = [0]
            ya_i = [0]

            for sc in range(NSC):
                g0 = sc * SC
                rank = g0 // TOK

                scp = sc % 2
                brep = (brep0, brep1)[scp]
                crep = (crep0, crep1)[scp]
                u_sl = (u_sl0, u_sl1)[scp]
                sr_sl = (sr_sl0, sr_sl1)[scp]
                dtr_sl = (dtr0, dtr1)[scp]

                @blk.gpsimd
                def _(g):
                    if sc == 0:
                        g.wait_ge(sems["CC"], cc_ar)
                    g.wait_ge(sems["VS"], vs_rep_free[scp])
                    inc(g.dma_start(out=brep.ap(), in_=xdbc_out[128:144, g0:g0 + SC].partition_broadcast(128)),
                        "DG", dma=True)
                    inc(g.dma_start(out=crep.ap(), in_=xdbc_out[144:160, g0:g0 + SC].partition_broadcast(128)),
                        "DG", dma=True)
                    inc(g.dma_start(out=u_sl[:, :, :], in_=u_sp[:, :, g0:g0 + SC]), "DG", dma=True)
                    inc(g.dma_start(out=sr_sl[:, :, :], in_=sr_sp[:, :, g0:g0 + SC]), "DG", dma=True)
                    inc(g.dma_start(out=dtr_sl[:, :], in_=xdbc_out[0:128, g0:g0 + SC]), "DG", dma=True)
                ld_sc = cnt["DG"]

                if sc == 0 or sc == NSC // 2:
                    @blk.vector
                    def _(v):
                        v.memset(hcarry[:, :, :], 0.0)

                for ct in range(4):
                    delta = (delta0, delta1, delta2, delta3)[ct]

                    @blk.tensor
                    def _(t):
                        t.wait_ge(sems["SS"], dt_ps_ss[0])
                        t.wait_ge(sems["DG"], ld_sc)
                        inc(t.matmul(PSA[:, 0:SC], wdt_bf[:, ct * 128:(ct + 1) * 128],
                                     dtr_sl[:, :], start=True, stop=True), "TS")
                    ts_dt = cnt["TS"]

                    @blk.scalar
                    def _(s):
                        s.wait_ge(sems["TS"], ts_dt)
                        s.wait_ge(sems["VS"], vs_sp_free)
                        sedge(s, s.activation(out=spa[:, :], in_=PSA[:, 0:SC], func=AF.Abs, bias=bdt_col[:, ct:ct + 1]))
                        sedge(s, s.activation(out=spa[:, :], in_=spa[:, :], func=AF.Exp, scale=negone_col[:, 0:1]))
                        sedge(s, s.activation(out=spa[:, :], in_=spa[:, :], func=AF.Ln, bias=ones_col[:, 0:1]))
                        i = s.activation(out=spb[:, :], in_=PSA[:, 0:SC], func=AF.Relu, bias=bdt_col[:, ct:ct + 1])
                        inc(i, "SS")
                    dt_ps_ss[0] = cnt["SS"]
                    ss_sp = cnt["SS"]

                    @blk.vector
                    def _(v):
                        v.wait_ge(sems["SS"], ss_sp)
                        v.wait_ge(sems["SS"], ss_dA_ct[ct])
                        inc(v.tensor_add(delta[:, :], spa[:, :], spb[:, :]), "VS")
                        v.wait_ge(sems["VS"], cnt["VS"])
                    vs_delta_ct[ct] = cnt["VS"]
                    vs_sp_free = vs_delta_ct[ct]

                for ct in range(4):
                    par = (sc * 4 + ct) % 2
                    delta = (delta0, delta1, delta2, delta3)[ct]
                    dA = (dA0, dA1)[par]

                    @blk.scalar
                    def _(s):
                        s.wait_ge(sems["VS"], vs_delta_ct[ct])
                        s.wait_ge(sems["VS"], vs_dA_free[par])
                        for n in range(NST):
                            i = s.activation(out=dA[:, n, :], in_=delta[:, :], func=AF.Exp,
                                             scale=a_col[:, ct, n:n + 1])
                        inc(i, "SS")
                    ss_dA = cnt["SS"]
                    ss_dA_ct[ct] = ss_dA

                    @blk.vector
                    def _(v):
                        v.wait_ge(sems["DG"], ld_sc)
                        vedge(v, v.tensor_mul(du_s[:, :], delta[:, :], u_sl[:, ct, :]))
                        v.tensor_tensor(
                            dBu.ap(),
                            du_s.ap().rearrange("p (o t) -> p o t", o=1).broadcast_to([128, NST, SC]),
                            brep.ap(), OP.mult)
                        v.wait_ge(sems["SS"], ss_dA)
                        for n in range(NST):
                            i = v.tensor_tensor_scan(out=h_s[:, n, :], data0=dA[:, n, :], data1=dBu[:, n, :],
                                                     initial=hcarry[:, ct, n:n + 1], op0=OP.mult, op1=OP.add)
                        vedge(v, i)
                        v.tensor_copy(hcarry[:, ct, :],
                                      h_s.ap()[:, :, SC - 1:SC].rearrange("p n o -> p (n o)"))
                        v.tensor_tensor(dA.ap(), h_s.ap(), crep.ap(), OP.mult)
                        v.tensor_add(dA.ap()[:, 0:8, :].rearrange("p n t -> p (n t)"),
                                     dA.ap()[:, 0:8, :].rearrange("p n t -> p (n t)"),
                                     dA.ap()[:, 8:16, :].rearrange("p n t -> p (n t)"))
                        v.tensor_add(dA.ap()[:, 0:4, :].rearrange("p n t -> p (n t)"),
                                     dA.ap()[:, 0:4, :].rearrange("p n t -> p (n t)"),
                                     dA.ap()[:, 4:8, :].rearrange("p n t -> p (n t)"))
                        v.tensor_add(dA.ap()[:, 0:2, :].rearrange("p n t -> p (n t)"),
                                     dA.ap()[:, 0:2, :].rearrange("p n t -> p (n t)"),
                                     dA.ap()[:, 2:4, :].rearrange("p n t -> p (n t)"))
                        v.tensor_add(dA[:, 0, :], dA[:, 0, :], dA[:, 1, :])
                        ya_st = (ya0, ya1)[ya_i[0] % 2]
                        v.wait_ge(sems["DG"], ya_dma_ms[ya_i[0] % 2])
                        vedge(v, v.scalar_tensor_tensor(out=dA[:, 1, :], in0=u_sl[:, ct, :],
                                               scalar=d_col[:, ct:ct + 1], in1=dA[:, 0, :],
                                               op0=OP.mult, op1=OP.add))
                        inc(v.tensor_tensor(ya_st[:, :], dA[:, 1, :], sr_sl[:, ct, :], OP.mult), "VS")
                    vs_epi = cnt["VS"]
                    vs_dA_free[par] = vs_epi

                    @blk.gpsimd
                    def _(g):
                        g.wait_ge(sems["VS"], vs_epi)
                        ya_st = (ya0, ya1)[ya_i[0] % 2]
                        inc(g.dma_start(out=ya_in[rank * CH + ct * 128: rank * CH + (ct + 1) * 128,
                                                  (g0 % TOK):(g0 % TOK) + SC],
                                        in_=ya_st[:, :]), "DG", dma=True)
                    ya_dma_ms[ya_i[0] % 2] = cnt["DG"]
                    ya_i[0] += 1
                vs_rep_free[scp] = cnt["VS"]

            scan_st = cnt["DG"]

        with nc.Block() as blk:
            @blk.gpsimd
            def _(g):
                g.wait_ge(sems["DG"], scan_st)
                with allow_ici_all_to_all():
                    inc(g.collective_compute("AllToAll", OP.bypass, replica_groups=[list(range(R))],
                                             ins=[ya_in.ap()], outs=[ya_out.ap()]), "CC")
            cc_a2a = cnt["CC"]

            if DBG:
                @blk.sync
                def _(sy):
                    sy.wait_ge(sems["CC"], cc_a2a)
                    inc(sy.dma_start(out=dbg["ya"], in_=ya_out.ap()), "DS", dma=True)
                    inc(sy.dma_start(out=dbg["wout"], in_=wout_bf_d.ap()), "DS", dma=True)
                    sy.wait_ge(sems["CC"], cc_wag)
                    inc(sy.dma_start(out=dbg["wug"], in_=wug_out.ap()), "DS", dma=True)
                    inc(sy.dma_start(out=dbg["wdn"], in_=wdn_out.ap()), "DS", dma=True)
                    inc(sy.dma_start(out=dbg["u"], in_=u_sp.ap()), "DS", dma=True)
                    inc(sy.dma_start(out=dbg["sr"], in_=sr_sp.ap()), "DS", dma=True)

        with (
            nc.sbuf_tensor("wstream", [128, 32, 512], BF16) as wstream,
            nc.sbuf_tensor("midar", [128, 64, 512], BF16) as midar,
            nc.sbuf_tensor("stA", [128, 512], F32) as stA,
            nc.sbuf_tensor("bufB", [128, 4, D], F32) as bufB,
            nc.sbuf_tensor("bufC", [128, 4, D], F32) as bufC,
            nc.sbuf_tensor("tmB", [128, D], F32) as tmB,
            nc.sbuf_tensor("xf_fm", [128, 16, 512], BF16) as xf_fm,
            nc.sbuf_tensor("sg0", [128, 512], BF16) as sg0,
            nc.sbuf_tensor("sg1", [128, 512], BF16) as sg1,
            nc.Block() as blk,
        ):
            wo_view = wstream.ap()
            wug_v0 = wstream.ap()[:, 0:16, 0:256]
            wug_v1 = wstream.ap()[:, 16:32, 0:256]
            wdn_v0 = wstream.ap()[:, 0:32, :].rearrange("p a b -> p (a b)").rearrange(
                "p (kt q) -> p kt q", q=128)
            wdn_flat = wstream.ap().rearrange("p a b -> p (a b)")

            ya_kb0 = midar.ap()[:, 0:16, :]
            ya_kb1 = midar.ap()[:, 16:32, :]
            xf_bfview = bufC.ap().bitcast(BF16)

            dbg_stA_st = [0]
            ya_kb_ld = [0, 0]
            ya_kb_ts = [0, 0]
            wo_ld = [0]
            wo_ts = [0]
            stA_ts = [0]
            pse_ss = [0]
            psabcd_ss = [0, 0, 0, 0]
            psbf_half_ss = {0: 0, 1: 0}
            wug_ld = [0, 0]
            wug_ts = [0, 0]
            sg_vs = [0, 0]
            psug_done_v = [0, 0]
            psug_done_s = [0, 0]
            wdn_ld = [0, 0]
            wdn_ts = [0, 0]
            pscd_ss = [0, 0]
            bufB_free_vs = [0]
            bufC_free_st = [0]
            out_st_f = [0]
            midar_ts = [0]
            xf_fm_mm_ts = [0]

            def emit_stats(buf, wait_ss=None):
                @blk.vector
                def _(v):
                    if wait_ss is not None:
                        v.wait_ge(sems["SS"], wait_ss)
                    for tt in range(4):
                        i = v.reduce_sum(sums_c[:, tt:tt + 1], buf[:, tt, :], axis=AX.X)
                    inc(i, "VS")
                vs_r = cnt["VS"]

                @blk.scalar
                def _(s):
                    s.wait_ge(sems["VS"], vs_r)
                    for tt in range(4):
                        i = s.activation(out=sqs[:, :], in_=buf[:, tt, :], func=AF.Square,
                                         accum_out=ssq_c[:, tt:tt + 1])
                    inc(i, "SS")
                ss_q = cnt["SS"]

                @blk.vector
                def _(v):
                    v.wait_ge(sems["SS"], ss_q)
                    vedge(v, v.tensor_scalar(out=negmean_c[:, :], in0=sums_c[:, :], scalar1=-1.0 / D,
                                             scalar2=None, op0=OP.mult))
                    vedge(v, v.tensor_mul(t1_c[:, :], negmean_c[:, :], negmean_c[:, :]))
                    vedge(v, v.tensor_scalar(out=var_c[:, :], in0=ssq_c[:, :], scalar1=1.0 / D,
                                             scalar2=None, op0=OP.mult))
                    vedge(v, v.tensor_sub(var_c[:, :], var_c[:, :], t1_c[:, :]))
                    inc(v.tensor_scalar(out=t1_c[:, :], in0=var_c[:, :], scalar1=EPS,
                                        scalar2=None, op0=OP.add), "VS")
                vs_v = cnt["VS"]

                @blk.scalar
                def _(s):
                    s.wait_ge(sems["VS"], vs_v)
                    inc(s.activation(out=sroot_c[:, :], in_=t1_c[:, :], func=AF.Sqrt), "SS")
                ss_r = cnt["SS"]

                @blk.vector
                def _(v):
                    v.wait_ge(sems["SS"], ss_r)
                    vedge(v, v.reciprocal(stok_c[:, :], sroot_c[:, :]))
                    inc(v.tensor_mul(btok_c[:, :], negmean_c[:, :], stok_c[:, :]), "VS")
                return cnt["VS"]


            for tb in range(4):
                t0 = tb * 512

                for mtg in range(4):
                    @blk.sync
                    def _(sy):
                        if tb == 0 and mtg == 0:
                            sy.wait_ge(sems["CC"], cc_a2a)
                        sy.wait_ge(sems["TS"], wo_ts[0])
                        sy.wait_ge(sems["TS"], wdn_ts[0])
                        sy.wait_ge(sems["TS"], wdn_ts[1])
                        inc(sy.dma_start(
                            out=wo_view[:, :, :],
                            in_=wout_bf_d[:, mtg * 512:(mtg + 1) * 512].rearrange(
                                "(kt p) m -> p kt m", p=128)), "DS", dma=True)
                    wo_ld[0] = cnt["DS"]

                    for kh in range(2):
                        ya_kb = (ya_kb0, ya_kb1)[kh]

                        @blk.gpsimd
                        def _(g):
                            if tb == 0 and mtg == 0:
                                g.wait_ge(sems["CC"], cc_a2a)
                            g.wait_ge(sems["TS"], ya_kb_ts[kh])
                            g.wait_ge(sems["TS"], midar_ts[0])
                            inc(g.dma_start(
                                out=ya_kb[:, :, :],
                                in_=ya_out[kh * 2048:(kh + 1) * 2048, t0:t0 + 512].rearrange(
                                    "(kt p) t -> p kt t", p=128)), "DG", dma=True)
                        ya_kb_ld[kh] = cnt["DG"]

                        @blk.tensor
                        def _(t):
                            t.wait_ge(sems["DG"], ya_kb_ld[kh])
                            t.wait_ge(sems["DS"], wo_ld[0])
                            if kh == 0:
                                for mt in range(4):
                                    t.wait_ge(sems["SS"], psabcd_ss[mt])
                                t.wait_ge(sems["VS"], psug_done_v[0])
                                t.wait_ge(sems["VS"], psug_done_v[1])
                                t.wait_ge(sems["SS"], psug_done_s[0])
                                t.wait_ge(sems["SS"], psug_done_s[1])
                                t.wait_ge(sems["SS"], pscd_ss[0])
                                t.wait_ge(sems["SS"], pscd_ss[1])
                            for mt in range(4):
                                ps = (PSA, PSB, PSC, PSD)[mt]
                                for kt in range(16):
                                    i = t.matmul(ps[:, :], wo_view[:, kh * 16 + kt, mt * 128:(mt + 1) * 128],
                                                 ya_kb[:, kt, :],
                                                 start=(kh == 0 and kt == 0), stop=(kh == 1 and kt == 15))
                            inc(i, "TS")
                        ya_kb_ts[kh] = cnt["TS"]
                    wo_ts[0] = cnt["TS"]
                    ts_mm4 = cnt["TS"]

                    for mt in range(4):
                        gmt = mtg * 4 + mt
                        ps = (PSA, PSB, PSC, PSD)[mt]

                        @blk.scalar
                        def _(s):
                            s.wait_ge(sems["TS"], ts_mm4)
                            s.wait_ge(sems["TS"], stA_ts[0])
                            if tb >= 1 and mtg == 0 and mt == 0:
                                s.wait_ge(sems["VS"], bufB_free_vs[0])
                            if DBG:
                                s.wait_ge(sems["DS"], dbg_stA_st[0])
                            i = s.copy(out=stA[:, :], in_=ps[:, :])
                            psabcd_ss[mt] = inc(i, "SS")
                        ss_stA = cnt["SS"]

                        if DBG:
                            @blk.sync
                            def _(sy):
                                sy.wait_ge(sems["SS"], ss_stA)
                                inc(sy.dma_start(out=dbg["stA"][tb, gmt], in_=stA[:, :]), "DS", dma=True)
                            dbg_stA_st[0] = cnt["DS"]

                        @blk.tensor
                        def _(t):
                            t.wait_ge(sems["SS"], ss_stA)
                            t.wait_ge(sems["SS"], pse_ss[0])
                            for tt in range(4):
                                i = t.transpose(PSE[:, tt * 128:(tt + 1) * 128],
                                                stA[:, tt * 128:(tt + 1) * 128], identf[:, :])
                            inc(i, "TS")
                        stA_ts[0] = cnt["TS"]
                        ts_tr = cnt["TS"]

                        @blk.scalar
                        def _(s):
                            s.wait_ge(sems["TS"], ts_tr)
                            i = s.copy(out=bufB.ap()[:, :, gmt * 128:(gmt + 1) * 128],
                                       in_=PSE.ap().rearrange("p (tt q) -> p tt q", q=128))
                            inc(i, "SS")
                        pse_ss[0] = cnt["SS"]
                ss_mo = cnt["SS"]

                if DBG:
                    @blk.sync
                    def _(sy):
                        sy.wait_ge(sems["SS"], ss_mo)
                        inc(sy.dma_start(out=dbg["mo"][t0:t0 + 512, :].rearrange("(tt p) d -> p tt d", p=128),
                                         in_=bufB[:, :, :]), "DS", dma=True)
                    dbg_mo_st = cnt["DS"]

                    @blk.vector
                    def _(v):
                        v.wait_ge(sems["DS"], dbg_mo_st)

                @blk.sync
                def _(sy):
                    sy.wait_ge(sems["DS"], bufC_free_st[0])
                    inc(sy.dma_start(out=bufC[:, :, :],
                                     in_=x_my[t0:t0 + 512, :].rearrange("(tt p) d -> p tt d", p=128)),
                        "DS", dma=True)
                ld_xs = cnt["DS"]

                @blk.vector
                def _(v):
                    v.wait_ge(sems["DS"], ld_xs)
                vs_tmp = emit_stats(bufC)

                @blk.vector
                def _(v):
                    v.wait_ge(sems["SS"], ss_mo)
                    for tt in range(4):
                        v.tensor_scalar(out=tmB[:, :], in0=bufC[:, tt, :],
                                        scalar1=negmean_c[:, tt:tt + 1], scalar2=stok_c[:, tt:tt + 1],
                                        op0=OP.add, op1=OP.mult)
                        i = v.tensor_add(bufB[:, tt, :], bufB[:, tt, :], tmB[:, :])
                    inc(i, "VS")

                vs_tmp = emit_stats(bufB)

                @blk.vector
                def _(v):
                    for tt in range(4):
                        v.tensor_scalar(out=tmB[:, :], in0=bufB[:, tt, :],
                                        scalar1=negmean_c[:, tt:tt + 1], scalar2=stok_c[:, tt:tt + 1],
                                        op0=OP.add, op1=OP.mult)
                        i = v.tensor_add(bufB[:, tt, :], bufC[:, tt, :], tmB[:, :])
                    inc(i, "VS")
                vs_x1 = cnt["VS"]

                if DBG:
                    @blk.sync
                    def _(sy):
                        sy.wait_ge(sems["VS"], vs_x1)
                        inc(sy.dma_start(out=dbg["x1"][t0:t0 + 512, :].rearrange("(tt p) d -> p tt d", p=128),
                                         in_=bufB[:, :, :]), "DS", dma=True)
                    dbg_x1_st = cnt["DS"]

                    @blk.vector
                    def _(v):
                        v.wait_ge(sems["DS"], dbg_x1_st)

                vs_tmp = emit_stats(bufB)

                @blk.scalar
                def _(s):
                    s.wait_ge(sems["VS"], vs_tmp)
                    for tt in range(4):
                        i = s.activation(out=xf_bfview[:, tt, 0:D], in_=bufB[:, tt, :], func=AF.Identity,
                                         scale=stok_c[:, tt:tt + 1], bias=btok_c[:, tt:tt + 1])
                    inc(i, "SS")
                ss_xf = cnt["SS"]

                for tt in range(4):
                    for hh in range(2):
                        half = (tt * 2 + hh) % 2

                        @blk.tensor
                        def _(t):
                            if tt == 0 and hh == 0:
                                t.wait_ge(sems["SS"], ss_xf)
                                t.wait_ge(sems["TS"], xf_fm_mm_ts[0])
                            t.wait_ge(sems["SS"], psbf_half_ss[half])
                            for dt in range(hh * 8, hh * 8 + 8):
                                i = t.transpose(
                                    PSBF[:, half * 1024 + (dt - hh * 8) * 128:
                                         half * 1024 + (dt - hh * 8 + 1) * 128],
                                    xf_bfview[:, tt, dt * 128:(dt + 1) * 128], identb[:, :])
                            inc(i, "TS")
                        ts_xf = cnt["TS"]

                        @blk.scalar
                        def _(s):
                            s.wait_ge(sems["TS"], ts_xf)
                            for dt in range(hh * 8, hh * 8 + 8):
                                i = s.mul(out=xf_fm[:, dt, tt * 128:(tt + 1) * 128],
                                          in_=PSBF[:, half * 1024 + (dt - hh * 8) * 128:
                                                   half * 1024 + (dt - hh * 8 + 1) * 128],
                                          mul=suug_col[:, dt:dt + 1])
                            inc(i, "SS")
                        psbf_half_ss[half] = cnt["SS"]
                ss_xffm = cnt["SS"]

                if DBG:
                    @blk.sync
                    def _(sy):
                        sy.wait_ge(sems["SS"], ss_xffm)
                        inc(sy.dma_start(out=dbg["xf"][tb], in_=xf_fm[:, :, :]), "DS", dma=True)

                for mi in range(64):
                    wug_v = (wug_v0, wug_v1)[mi % 2]
                    rr = mi // 8
                    scol = (mi % 8) * 128

                    @blk.sync
                    def _(sy):
                        if mi < 2:
                            sy.wait_ge(sems["TS"], wo_ts[0])
                        sy.wait_ge(sems["TS"], wug_ts[mi % 2])
                        inc(sy.dma_start(out=wug_v[:, :, 0:128],
                                     in_=wug_out[rr * D:(rr + 1) * D, scol:scol + 128].rearrange(
                                         "(kt p) m -> p kt m", p=128)), "DS", dma=True)
                        inc(sy.dma_start(out=wug_v[:, :, 128:256],
                                         in_=wug_out[rr * D:(rr + 1) * D, 1024 + scol:1024 + scol + 128].rearrange(
                                             "(kt p) m -> p kt m", p=128)), "DS", dma=True)
                    wug_ld[mi % 2] = cnt["DS"]

                    psU = (PSA, PSC)[mi % 2]
                    psG = (PSB, PSD)[mi % 2]

                    @blk.tensor
                    def _(t):
                        t.wait_ge(sems["DS"], wug_ld[mi % 2])
                        if mi == 0:
                            t.wait_ge(sems["SS"], ss_xffm)
                        t.wait_ge(sems["VS"], psug_done_v[mi % 2])
                        t.wait_ge(sems["SS"], psug_done_s[mi % 2])
                        for kt in range(16):
                            t.matmul(psU[:, :], wug_v[:, kt, 0:128], xf_fm[:, kt, :],
                                     start=(kt == 0), stop=(kt == 15))
                        for kt in range(16):
                            i = t.matmul(psG[:, :], wug_v[:, kt, 128:256], xf_fm[:, kt, :],
                                         start=(kt == 0), stop=(kt == 15))
                        inc(i, "TS")
                    wug_ts[mi % 2] = cnt["TS"]
                    ts_ug = cnt["TS"]

                    sg = (sg0, sg1)[mi % 2]

                    @blk.scalar
                    def _(s):
                        s.wait_ge(sems["TS"], ts_ug)
                        s.wait_ge(sems["VS"], sg_vs[mi % 2])
                        i = s.activation(out=sg[:, :], in_=psG[:, :], func=AF.Silu)
                        inc(i, "SS")
                    psug_done_s[mi % 2] = cnt["SS"]
                    ss_sg = cnt["SS"]

                    @blk.vector
                    def _(v):
                        v.wait_ge(sems["SS"], ss_sg)
                        i = v.scalar_tensor_tensor(out=midar[:, mi, :], in0=psU[:, :],
                                                   scalar=sudp_col[:, mi:mi + 1], in1=sg[:, :],
                                                   op0=OP.mult, op1=OP.mult)
                        inc(i, "VS")
                    psug_done_v[mi % 2] = cnt["VS"]
                    sg_vs[mi % 2] = cnt["VS"]
                vs_mid = cnt["VS"]

                if DBG:
                    @blk.sync
                    def _(sy):
                        sy.wait_ge(sems["VS"], vs_mid)
                        inc(sy.dma_start(out=dbg["mid"][tb], in_=midar[:, :, :]), "DS", dma=True)
                    dbg_mid_st = cnt["DS"]

                    @blk.gpsimd
                    def _(g):
                        g.wait_ge(sems["DS"], dbg_mid_st)

                for dmt in range(16):
                    rr = dmt // 2
                    c0 = (dmt % 2) * 128
                    wdn_v = wdn_flat[:, (dmt % 2) * 8192:(dmt % 2) * 8192 + 8192].rearrange(
                        "p (kt q) -> p kt q", q=128)
                    ps = (PSC, PSD)[dmt % 2]

                    @blk.sync
                    def _(sy):
                        if dmt < 2:
                            sy.wait_ge(sems["TS"], wug_ts[0])
                            sy.wait_ge(sems["TS"], wug_ts[1])
                        sy.wait_ge(sems["TS"], wdn_ts[dmt % 2])
                        inc(sy.dma_start(out=wdn_v,
                                         in_=wdn_out[rr * INTER:(rr + 1) * INTER, c0:c0 + 128].rearrange(
                                             "(kt p) m -> p kt m", p=128)), "DS", dma=True)
                    wdn_ld[dmt % 2] = cnt["DS"]

                    @blk.tensor
                    def _(t):
                        t.wait_ge(sems["DS"], wdn_ld[dmt % 2])
                        if dmt == 0:
                            t.wait_ge(sems["VS"], vs_mid)
                            t.wait_ge(sems["VS"], psug_done_v[1])
                            t.wait_ge(sems["SS"], psug_done_s[1])
                        t.wait_ge(sems["SS"], pscd_ss[dmt % 2])
                        for kt in range(64):
                            i = t.matmul(ps[:, :], wdn_v[:, kt, :], midar[:, kt, :],
                                         start=(kt == 0), stop=(kt == 63))
                        inc(i, "TS")
                    wdn_ts[dmt % 2] = cnt["TS"]
                    ts_dn = cnt["TS"]

                    @blk.scalar
                    def _(s):
                        s.wait_ge(sems["TS"], ts_dn)
                        s.wait_ge(sems["TS"], stA_ts[0])
                        if dmt == 0:
                            s.wait_ge(sems["DS"], out_st_f[0])
                        i = s.copy(out=stA[:, :], in_=ps[:, :])
                        pscd_ss[dmt % 2] = inc(i, "SS")
                    ss_dnA = cnt["SS"]

                    @blk.tensor
                    def _(t):
                        t.wait_ge(sems["SS"], ss_dnA)
                        t.wait_ge(sems["SS"], pse_ss[0])
                        for tt in range(4):
                            i = t.transpose(PSE[:, tt * 128:(tt + 1) * 128],
                                            stA[:, tt * 128:(tt + 1) * 128], identf[:, :])
                        inc(i, "TS")
                    stA_ts[0] = cnt["TS"]
                    ts_tr2 = cnt["TS"]

                    @blk.scalar
                    def _(s):
                        s.wait_ge(sems["TS"], ts_tr2)
                        i = s.copy(out=bufC.ap()[:, :, dmt * 128:(dmt + 1) * 128],
                                   in_=PSE.ap().rearrange("p (tt q) -> p tt q", q=128))
                        inc(i, "SS")
                    pse_ss[0] = cnt["SS"]
                midar_ts[0] = cnt["TS"]
                xf_fm_mm_ts[0] = wug_ts[1] if tb >= 0 else 0
                ss_fo = cnt["SS"]

                if DBG:
                    @blk.sync
                    def _(sy):
                        sy.wait_ge(sems["SS"], ss_fo)
                        inc(sy.dma_start(out=dbg["fo"][t0:t0 + 512, :].rearrange("(tt p) d -> p tt d", p=128),
                                         in_=bufC[:, :, :]), "DS", dma=True)
                    dbg_fo_st = cnt["DS"]

                    @blk.vector
                    def _(v):
                        v.wait_ge(sems["DS"], dbg_fo_st)

                vs_tmp = emit_stats(bufC, wait_ss=ss_fo)

                @blk.vector
                def _(v):
                    v.wait_ge(sems["SS"], ss_fo)
                    for tt in range(4):
                        v.tensor_scalar(out=tmB[:, :], in0=bufC[:, tt, :],
                                        scalar1=negmean_c[:, tt:tt + 1], scalar2=stok_c[:, tt:tt + 1],
                                        op0=OP.add, op1=OP.mult)
                        i = v.tensor_add(bufC[:, tt, :], bufB[:, tt, :], tmB[:, :])
                    inc(i, "VS")
                bufB_free_vs[0] = cnt["VS"]
                vs_out = cnt["VS"]

                @blk.sync
                def _(sy):
                    sy.wait_ge(sems["VS"], vs_out)
                    inc(sy.dma_start(out=out_tok[t0:t0 + 512, :].rearrange("(tt p) d -> p tt d", p=128),
                                     in_=bufC[:, :, :]), "DS", dma=True)
                bufC_free_st[0] = cnt["DS"]
                out_st_f[0] = cnt["DS"]

            final_st = cnt["DS"]
            final_dg = cnt["DG"]

            @blk.sync
            def _(sy):
                sy.wait_ge(sems["DS"], final_st)
                sy.wait_ge(sems["DG"], final_dg)

    return nc


_NC_CACHE = {}


def kernel(**inputs):
    x = np.ascontiguousarray(np.asarray(inputs["L0_attn_x"], dtype=np.float32).reshape(T, D))
    w_norm = np.asarray(inputs["L0_attn_w_norm"], dtype=np.float32)
    W_in = np.asarray(inputs["L0_attn_W_in"], dtype=np.float32)
    conv_w = np.asarray(inputs["L0_attn_conv_w"], dtype=np.float32)
    conv_b = np.asarray(inputs["L0_attn_conv_b"], dtype=np.float32)
    W_xp = np.asarray(inputs["L0_attn_W_xp"], dtype=np.float32)
    W_dt = np.asarray(inputs["L0_attn_W_dt"], dtype=np.float32)
    b_dt = np.asarray(inputs["L0_attn_b_dt"], dtype=np.float32)
    A_log = np.asarray(inputs["L0_attn_A_log"], dtype=np.float32)
    Dv = np.asarray(inputs["L0_attn_D"], dtype=np.float32)
    W_out = np.asarray(inputs["L0_attn_W_out"], dtype=np.float32)
    SU_ug = np.asarray(inputs["L0_ffn_SU_ug"], dtype=np.float32)
    SU_dp = np.asarray(inputs["L0_ffn_SU_dp"], dtype=np.float32)
    Ws_ug = np.asarray(inputs["L0_ffn_Wscale_ug"], dtype=np.float32)
    Ws_dp = np.asarray(inputs["L0_ffn_Wscale_dp"], dtype=np.float32)
    W_up = np.asarray(inputs["L0_ffn_W_up"], dtype=np.float32)
    W_gate = np.asarray(inputs["L0_ffn_W_gate"], dtype=np.float32)
    W_down = np.asarray(inputs["L0_ffn_W_down"], dtype=np.float32)

    identb = np.eye(128, dtype=np.float32).astype(ml_dtypes.bfloat16)
    identf = np.eye(128, dtype=np.float32)

    if "nc" not in _NC_CACHE:
        _NC_CACHE["nc"] = build_nc()
    nc = _NC_CACHE["nc"]

    def colN(v, pad=None):
        n = v.shape[0] // 128
        out = np.ascontiguousarray(v.reshape(n, 128).T)
        if pad is not None and n < pad:
            out = np.concatenate([out, np.zeros((128, pad - n), np.float32)], axis=1)
        return out

    def col4(v):
        return colN(v)

    in_maps = []
    for c in range(R):
        ch0 = c * CH
        im = {
            "x_full": x,
            "x_my": np.ascontiguousarray(x[c * TOK:(c + 1) * TOK]),
            "w_in": np.ascontiguousarray(
                np.concatenate([W_in[:, ch0:ch0 + CH], W_in[:, DI + ch0:DI + ch0 + CH]], axis=1)),
            "conv_w": np.ascontiguousarray(
                conv_w[:, ch0:ch0 + CH].reshape(4, 4, 128).transpose(2, 1, 0)),
            "conv_b": col4(conv_b[ch0:ch0 + CH]),
            "w_xp": np.ascontiguousarray(W_xp[ch0:ch0 + CH]),
            "w_dt": np.ascontiguousarray(W_dt[:, ch0:ch0 + CH]),
            "b_dt": col4(b_dt[ch0:ch0 + CH]),
            "a_log": np.ascontiguousarray(A_log[ch0:ch0 + CH].reshape(4, 128, NST).transpose(1, 0, 2)),
            "d_vec": col4(Dv[ch0:ch0 + CH]),
            "w_norm": colN(w_norm),
            "w_out": W_out,
            "w_up_sh": np.ascontiguousarray(W_up[c * (INTER // R):(c + 1) * (INTER // R)]),
            "w_gate_sh": np.ascontiguousarray(W_gate[c * (INTER // R):(c + 1) * (INTER // R)]),
            "w_down_sh": np.ascontiguousarray(W_down[c * (D // R):(c + 1) * (D // R)]),
            "ws_ug_sh": colN(Ws_ug[c * (INTER // R):(c + 1) * (INTER // R)]),
            "ws_gt_sh": colN(Ws_ug[INTER + c * (INTER // R):INTER + (c + 1) * (INTER // R)], pad=8),
            "ws_dp_sh": colN(Ws_dp[c * (D // R):(c + 1) * (D // R)], pad=8),
            "su_ug": colN(SU_ug),
            "su_dp": colN(SU_dp),
            "identb": identb,
            "identf": identf,
        }
        in_maps.append(im)

    res = run_bass_kernel_spmd(nc, in_maps, core_ids=list(range(R)))
    out = np.concatenate([res.results[c]["out_tok"] for c in range(R)], axis=0)
    if DBG:
        kernel.last_results = res
    return out.reshape(2, 8192, 2048)
